# version 15
# baseline (speedup 1.0000x reference)
from concourse.bass_utils import run_bass_kernel_spmd

"""Megakernel builder: mk_sandwich_d1 RWKV block, feature-major, 8-core SPMD.

Layout: channels on partitions in G=8 groups of 128; tokens on the free axis.
Each core owns (batch b = core//2, half = core%2) -> 2048 tokens; every core
also runs the k/v/scan prefix pipeline over the 2048 "prefix" tokens (gated to
a no-op on even cores via lngate = -1e30 so exp(k - m + lngate) = 0).

See /root/problem/NOTES.md for the full design.
"""
import numpy as np
import concourse.bass as bass
import concourse.mybir as mybir

F32 = mybir.dt.float32
BF16 = mybir.dt.bfloat16
AF = mybir.ActivationFunctionType
OP = mybir.AluOpType

G = 8
FG = 32
C = 1024
F = 4096
TOWN = 2048
EPS = 1e-5

_CUSTOM = {}


def _np_lvl(r, c):
    return r - (r >= c) * c


def make_custom_ops():
    global _CUSTOM
    if _CUSTOM:
        return _CUSTOM
    from concourse.dve_spec import Spec, Src0, Src1, C0, C1, C2, lower
    from concourse.dve_op import DveOp
    from concourse.dve_table_gen import dve_ver_for
    import concourse.dve_ops as dmod

    def lvl(r, c):
        return r - (r >= c) * c

    def mk(name, body, reference):
        ver = dve_ver_for("TRN2")
        sha = lower(Spec(body=body), ver=ver, perf_en=False, name=name).sha(ver)
        return DveOp(name, Spec(body=body), subdim=False, uops_sha={ver: sha},
                     reference=reference)

    ops = {}
    ops["ATH_TERN"] = mk("ATH_TERN", (Src0 >= C0) - (C1 >= Src0),
        lambda in0, in1, s0, s1, imm2: (in0 >= s0) * 1.0 - (in0 <= s1))
    ops["ATH_RINT"] = mk("ATH_RINT", (Src0 + C0) - C1,
        lambda in0, in1, s0, s1, imm2: np.rint(in0))

    names = [o.name for o in dmod.OPS]
    repl = [i for i, n in enumerate(names) if n.startswith("DVEHASH32L")]
    have = set(names)
    for op in ops.values():
        if op.name in have:
            continue
        if repl:
            dmod.OPS[repl.pop(0)] = op
        else:
            dmod.OPS.append(op)
    assert len(dmod.OPS) <= 31
    dmod._SUB_OPCODE_FOR_NAME.clear()
    dmod._SUB_OPCODE_FOR_NAME.update(
        {op.name: dmod._CUSTOM_DVE_ROW_BASE + i for i, op in enumerate(dmod.OPS)})
    dmod.CUSTOM_DVE_SPECS.clear()
    dmod.CUSTOM_DVE_SPECS.update(
        {op.name: op.spec for op in dmod.OPS if op.spec is not None})
    dmod.CUSTOM_DVE_REFS.clear()
    dmod.CUSTOM_DVE_REFS.update({op.name: op.ref for op in dmod.OPS})
    _CUSTOM = ops
    return ops


ENGINES = ("sync", "vector", "scalar", "tensor", "gpsimd")
DMA_LANES = 4
DMA_QUEUES = {"sync": "dS", "gpsimd": "dG", "scalar": "dA"}
ALL_DMA_SEMS = [q + str(i) for q in ("dS", "dG", "dA") for i in range(DMA_LANES)]


class Buf:
    __slots__ = ("ap_fn", "ready", "readers", "name", "track")

    def __init__(self, ap_fn, name="", track=None):
        self.ap_fn = ap_fn
        self.ready = {}
        self.readers = {}
        self.name = name
        self.track = track

    def ap(self):
        return self.ap_fn() if callable(self.ap_fn) else self.ap_fn


class Prog:
    serial = False
    serial_engines = set()

    def __init__(self, nc):
        self.nc = nc
        self.q = {e: [] for e in ENGINES}
        self.cnt = {s: 0 for s in ALL_DMA_SEMS}
        self.cnt.update({"v": 0, "a": 0, "t": 0, "p": 0, "sq": 0})
        self.esem = {"sync": "sq", "vector": "v", "scalar": "a",
                     "tensor": "t", "gpsimd": "p"}
        self.dma_issue = {"dS": 0, "dG": 0, "dA": 0}

    def op(self, eng, fn, reads=(), writes=(), dma=False, tag=""):
        waits = {}
        if dma:
            qn = DMA_QUEUES[eng]
            lane = self.dma_issue[qn] % DMA_LANES
            self.dma_issue[qn] += 1
            my_sem = qn + str(lane)
        else:
            my_sem = self.esem[eng]
        reads = [b.track or b for b in reads]
        writes = [b.track or b for b in writes]
        skip_own = (eng == "tensor") and not dma
        for b in reads:
            for s, c in b.ready.items():
                if skip_own and s == "t":
                    continue
                waits[s] = max(waits.get(s, 0), c)
        for b in writes:
            for src in (b.ready, b.readers):
                for s, c in src.items():
                    if skip_own and s == "t":
                        continue
                    waits[s] = max(waits.get(s, 0), c)
        if self.serial or eng in self.serial_engines:
            waits = {s: c for s, c in self.cnt.items() if c > 0}
        if dma and self.cnt[my_sem] > 0:
            waits[my_sem] = max(waits.get(my_sem, 0), self.cnt[my_sem])
        inc = 16 if dma else 1
        self.cnt[my_sem] += inc
        tok = (my_sem, self.cnt[my_sem])
        for b in writes:
            b.ready = {tok[0]: tok[1]}
            b.readers = {}
        for b in reads:
            b.readers[tok[0]] = tok[1]
        self.q[eng].append((fn, waits, my_sem, inc, tag))

    def barrier(self):
        snap = dict(self.cnt)
        for e in ENGINES:
            self.q[e].append((None, {s: c for s, c in snap.items() if c > 0},
                              None, 0, "barrier"))

    def emit(self, block, sems):
        emitted_waits = {e: {} for e in ENGINES}

        def run(eng_name, eng):
            my_csem = self.esem[eng_name]
            for fn, waits, my_sem, inc, tag in self.q[eng_name]:
                for s, c in sorted(waits.items()):
                    w = emitted_waits[eng_name]
                    if w.get(s, 0) >= c:
                        continue
                    eng.wait_ge(sems[s], c)
                    w[s] = c
                if fn is None:
                    continue
                inst = fn(eng)
                if inc:
                    assert inst is not None, f"op returned no instruction: {tag}"
                    inst.then_inc(sems[my_sem], inc)

        for ename in ENGINES:
            dec = getattr(block, ename)
            dec(lambda eng, _n=ename: run(_n, eng))



def ceil_div(a, b):
    return (a + b - 1) // b


class MK:
    def __init__(self, debug_dump=()):
        self.ops = make_custom_ops()
        self.debug_dump = set(debug_dump)
        self.nc = bass.Bass("TRN2", target_bir_lowering=False, debug=False)
        self.inputs = {}
        self.outputs = {}
        self.sb = {}
        self._sb_tensors = []
        self._consts = {}
        self._dbg = {}

    def dram_in(self, name, shape, dtype=F32):
        t = self.nc.dram_tensor(name, list(shape), dtype, kind="ExternalInput")
        self.inputs[name] = t
        return t.ap()

    def dram_out(self, name, shape, dtype=F32):
        t = self.nc.dram_tensor(name, list(shape), dtype, kind="ExternalOutput")
        self.outputs[name] = t
        return t.ap()

    def dram_tmp(self, name, shape, dtype=F32):
        return self.nc.dram_tensor(name, list(shape), dtype).ap()

    def sbuf(self, name, shape, dtype=F32):
        t = self.nc.alloc_sbuf_tensor(name, list(shape), dtype)
        self._sb_tensors.append(t)
        return Buf(t.ap(), name)

    def psum(self, name, shape, dtype=F32):
        t = self.nc.alloc_psum_tensor(name, list(shape), dtype)
        return Buf(t.ap(), name)

    def dbg(self, name, buf_ap, shape, dtype=F32):
        o = self.dram_out("dbg_" + name, shape, dtype)
        return o

    def constf(self, value):
        key = float(value)
        if key in self._consts:
            return self._consts[key]
        b = self.sbuf(f"cst_{len(self._consts)}", (128, 1))
        self.p.op("vector", lambda e: e.memset(b.ap(), key), writes=[b])
        self.nc.const_aps.register(F32, key, b.ap())
        self._consts[key] = b
        return b

    def build(self, stop_after="all"):
        nc = self.nc
        p = Prog(nc)
        self.p = p
        TT = 256
        NT = TOWN // TT
        NP = NT
        self.p = p

        x_pre = self.dram_in("x_pre", (C, TOWN))
        x_own = self.dram_in("x_own", (C, TOWN))
        iden = self.dram_in("iden", (128, 128))
        cvec = self.dram_in("cvec", (128, 8 * 16))
        gates = self.dram_in("gates", (128, G))
        bgate = self.dram_in("bgate", (128, G))
        wkT = self.dram_in("wkT", (C, C))
        wvT = self.dram_in("wvT", (C, C))
        wrT = self.dram_in("wrT", (C, C))
        woT = self.dram_in("woT", (C, C))
        wfkT = self.dram_in("wfkT", (C, F))
        wfvT = self.dram_in("wfvT", (F, C))
        wfrT = self.dram_in("wfrT", (C, C))
        wgT = self.dram_in("wgT", (C, F))
        wuT = self.dram_in("wuT", (C, F))
        wdT = self.dram_in("wdT", (F, C))

        out = self.dram_out("out", (C, TOWN))

        x1_s = self.dram_tmp("x1_s", (C, TOWN + 1))
        xk2_s = self.dram_tmp("xk2_s", (C, TOWN), BF16)
        sr2_s = self.dram_tmp("sr2_s", (C, TOWN), BF16)
        out1_s = self.dram_tmp("out1_s", (C, TOWN))
        xhat1_s = self.dram_tmp("xhat1_s", (C, TOWN), BF16)
        gpre_s = self.dram_tmp("gpre_s", (F, TOWN), BF16)
        kk_s = self.dram_tmp("kk_s", (F, TOWN), BF16)
        xq2_s = self.dram_tmp("xq2_s", (F, TOWN), BF16)
        qrow_s = self.dram_tmp("qrow_s", (1, TOWN))
        x1_s3 = x1_s.rearrange("(g p) t -> p g t", p=128)

        ones_bf = self.sbuf("ones_bf", (128, 128), BF16)
        ident = self.sbuf("ident", (128, 128))
        cv = self.sbuf("cv", (128, 8 * 16))
        gate_t = self.sbuf("gate_t", (128, G))
        bg_t = self.sbuf("bg_t", (128, G))
        lam = self.sbuf("lam", (128, G))
        zero8 = self.sbuf("zero8", (128, G))

        def cvs(i):
            return cv.ap()[:, i * G:(i + 1) * G]
        CV = {n: i for i, n in enumerate(
            ["tmk1", "tmk", "tmv1", "tmv", "tmr1", "tmr", "g1", "b1",
             "g2", "b2", "ftmk1", "ftmk", "ftmr1", "ftmr", "td", "u"])}

        sems = {}
        self._sems = sems

        p.op("sync", lambda e: e.dma_start(out=ident.ap(), in_=iden), writes=[ident], dma=True)
        p.op("sync", lambda e: e.dma_start(out=cv.ap(), in_=cvec), writes=[cv], dma=True)
        p.op("sync", lambda e: e.dma_start(out=gate_t.ap(), in_=gates), writes=[gate_t], dma=True)
        p.op("sync", lambda e: e.dma_start(out=bg_t.ap(), in_=bgate), writes=[bg_t], dma=True)
        p.op("vector", lambda e: e.memset(ones_bf.ap(), 1.0), writes=[ones_bf])
        p.op("vector", lambda e: e.memset(zero8.ap(), 0.0), writes=[zero8])
        c_eps = self.constf(EPS)
        c_one = self.constf(1.0)
        t8 = self.sbuf("t8", (128, G))
        p.op("scalar", lambda e: e.activation(t8.ap(), cvs(CV["td"]), AF.Exp),
             reads=[cv], writes=[t8])
        p.op("scalar", lambda e: e.activation(lam.ap(), t8.ap(), AF.Exp, scale=-1.0),
             reads=[t8], writes=[lam])

        ware = self.sbuf("ware", (128, G, 4096), BF16)
        def wview(ap, nm):
            return Buf(ap, nm, track=ware)
        wk_sb = wview(ware.ap()[:, :, 0:C], "wk")
        wv_sb = wview(ware.ap()[:, :, C:2 * C], "wv")
        wr_sb = wview(ware.ap()[:, :, 2 * C:3 * C], "wr")
        wo_sb = wview(ware.ap()[:, :, 3 * C:4 * C], "wo")
        for w_sb, w_h in ((wk_sb, wkT), (wv_sb, wvT), (wr_sb, wrT), (wo_sb, woT)):
            p.op("gpsimd", lambda e, a=w_sb, b=w_h: e.dma_start(
                out=a.ap(), in_=b.rearrange("(g p) d -> p g d", p=128)),
                writes=[w_sb], dma=True)

        x0 = self.sbuf("x0", (128, G, TT))
        x0b = self.sbuf("x0b", (128, G, TT), BF16)
        xsqb = self.sbuf("xsqb", (128, G, TT), BF16)
        h = [self.sbuf("h0", (128, G, TT + 1), BF16),
             self.sbuf("h1", (128, G, TT + 1), BF16)]
        hind = [self.sbuf("hinA", (128, G, TT)), self.sbuf("hinB", (128, G, TT))]
        dd = self.sbuf("dd", (128, G, TT))
        big32 = self.sbuf("big32", (128, 32, TT), BF16)
        rwkb = Buf(big32.ap()[:, 0:G, :], "rwkb")
        xkb = Buf(big32.ap()[:, G:2 * G, :], "xkb")
        xvb = Buf(big32.ap()[:, 2 * G:3 * G, :], "xvb")
        xrb = Buf(big32.ap()[:, 3 * G:4 * G, :], "xrb")
        ktd = [self.sbuf("ktA", (128, G, TT)), self.sbuf("ktB", (128, G, TT))]
        vtd = [self.sbuf("vtA", (128, G, TT)), self.sbuf("vtB", (128, G, TT))]
        srd = [self.sbuf("srA", (128, G, TT)), self.sbuf("srB", (128, G, TT))]
        ek = self.sbuf("ek", (128, G, TT))
        ekv = self.sbuf("ekv", (128, G, TT))
        hin, kt, vt, sr = hind[0], ktd[0], vtd[0], srd[0]
        a_st = self.sbuf("a_st", (128, G))
        b_st = self.sbuf("b_st", (128, G))
        m_st = self.sbuf("m_st", (128, G))
        rmax = self.sbuf("rmax", (128, G))
        mnew = self.sbuf("mnew", (128, G))
        ebias = self.sbuf("ebias", (128, G))
        ubias = self.sbuf("ubias", (128, G))
        sdel = self.sbuf("sdel", (128, G))
        srsc = self.sbuf("srsc", (128, G))
        r_msq = self.sbuf("r_msq", (128, TT))
        r_v0 = self.sbuf("r_v0", (128, TT))
        r_L = self.sbuf("r_L", (128, TT))
        r_r0 = self.sbuf("r_r0", (128, TT))
        r_inv = r_msq
        r_v1 = r_L
        r_rho = self.sbuf("r_rho", (128, TT))
        r_mrho = self.sbuf("r_mrho", (128, TT))
        r_mr0 = r_v0

        ps_S = self.psum("ps_S", (128, TT))
        ps_Q = self.psum("ps_Q", (128, TT))
        ps_m0 = self.psum("ps_m0", (128, TT))
        ps_m1 = self.psum("ps_m1", (128, TT))
        ps_kv = self.psum("ps_kv", (128, G, TT))
        p4 = [Buf(ps_kv.ap()[:, 2 * j, :], f"p4_{j}") for j in range(4)]

        p.op("vector", lambda e: e.memset(a_st.ap(), 0.0), writes=[a_st])
        p.op("vector", lambda e: e.memset(b_st.ap(), 0.0), writes=[b_st])
        p.op("vector", lambda e: e.memset(m_st.ap(), -1e30), writes=[m_st])

        def bcast_g(rtile, shape3=(128, G, TT)):
            return rtile.ap().unsqueeze(1).broadcast_to(list(shape3))

        dbg_outs = {}
        dbg_dt = {"rwkv": BF16}
        for nm in ("h", "hpre", "hin", "k", "v", "sr", "wkv", "rwkv", "idn", "npre", "ek2"):
            if nm in self.debug_dump:
                dbg_outs[nm] = self.dram_out("dbg_" + nm, (C, TOWN),
                                             dbg_dt.get(nm, F32)).rearrange(
                    "(g p) t -> p g t", p=128)

        dbg_stage = ekv

        def dump(nm, buf, ap3, t0, dense=True):
            if nm not in dbg_outs:
                return
            if not dense:
                p.op("vector", lambda e, a=ap3: e.tensor_copy(dbg_stage.ap(), a),
                     reads=[buf], writes=[dbg_stage])
                buf, ap3 = dbg_stage, dbg_stage.ap()
            p.op("scalar", lambda e, a=ap3, t=t0: e.dma_start(
                out=dbg_outs[nm][:, :, t:t + TT], in_=a), reads=[buf], dma=True)

        x1_s3 = x1_s.rearrange("(g p) t -> p g t", p=128)

        def ph1_s1(i, pre, full, src, t0, hc, hp):
            p.op("sync", lambda e, s=src, t=t0: e.dma_start(
                out=x0.ap(), in_=s.rearrange("(g p) t -> p g t", p=128)[:, :, t:t + TT]),
                writes=[x0], dma=True)
            p.op("gpsimd", lambda e: e.tensor_copy(x0b.ap(), x0.ap()),
                 reads=[x0], writes=[x0b])
            p.op("gpsimd", lambda e: e.tensor_mul(xsqb.ap(), x0.ap(), x0.ap()),
                 reads=[x0], writes=[xsqb])
            def _stats(e):
                for q, ps in ((x0b, ps_S), (xsqb, ps_Q)):
                    for g in range(G):
                        inst = e.matmul(ps.ap(), ones_bf.ap(), q.ap()[:, g, :],
                                        start=(g == 0), stop=(g == G - 1))
                return inst
            p.op("tensor", _stats, reads=[x0b, xsqb, ones_bf], writes=[ps_S, ps_Q])
            p.op("scalar", lambda e: e.activation(r_msq.ap(), ps_S.ap(), AF.Square, scale=1.0 / C),
                 reads=[ps_S], writes=[r_msq])
            p.op("vector", lambda e: e.scalar_tensor_tensor(
                r_v0.ap(), ps_Q.ap(), 1.0 / C, r_msq.ap(), OP.mult, OP.subtract),
                reads=[ps_Q, r_msq], writes=[r_v0])
            p.op("scalar", lambda e: e.activation(r_L.ap(), r_v0.ap(), AF.Ln, bias=EPS),
                 reads=[r_v0, c_eps], writes=[r_L])
            p.op("scalar", lambda e: e.activation(r_r0.ap(), r_L.ap(), AF.Exp, scale=-0.5),
                 reads=[r_L], writes=[r_r0])
            p.op("scalar", lambda e: e.activation(r_inv.ap(), r_L.ap(), AF.Exp, scale=-1.0),
                 reads=[r_L], writes=[r_inv])
            p.op("vector", lambda e: e.tensor_mul(r_v1.ap(), r_v0.ap(), r_inv.ap()),
                 reads=[r_v0, r_inv], writes=[r_v1])
            p.op("scalar", lambda e: e.activation(r_L.ap(), r_v1.ap(), AF.Ln, bias=EPS),
                 reads=[r_v1, c_eps], writes=[r_L])
            p.op("scalar", lambda e: e.activation(r_v1.ap(), r_L.ap(), AF.Exp, scale=-0.5),
                 reads=[r_L], writes=[r_v1])
            p.op("vector", lambda e: e.tensor_mul(r_rho.ap(), r_r0.ap(), r_v1.ap()),
                 reads=[r_r0, r_v1], writes=[r_rho])
            p.op("vector", lambda e: e.scalar_tensor_tensor(
                r_mrho.ap(), ps_S.ap(), 1.0 / C, r_rho.ap(), OP.mult, OP.mult),
                reads=[ps_S, r_rho], writes=[r_mrho])
            if full:
                p.op("vector", lambda e: e.scalar_tensor_tensor(
                    r_mr0.ap(), ps_S.ap(), 1.0 / C, r_r0.ap(), OP.mult, OP.mult),
                    reads=[ps_S, r_r0], writes=[r_mr0])

        def ph1_s2(i, pre, full, src, t0, hc, hp):
            kt, vt, sr, hin = ktd[i % 2], vtd[i % 2], srd[i % 2], hind[i % 2]
            p.op("gpsimd", lambda e: e.tensor_mul(dd.ap(), x0.ap(), bcast_g(r_rho)),
                 reads=[x0, r_rho], writes=[dd])
            p.op("gpsimd", lambda e: e.tensor_sub(dd.ap(), dd.ap(), bcast_g(r_mrho)),
                 reads=[dd, r_mrho], writes=[dd])
            def _gb(e):
                for g in range(G):
                    inst = e.tensor_scalar(
                        hc.ap()[:, g, 1:TT + 1], dd.ap()[:, g, :],
                        cvs(CV["g1"])[:, g:g + 1], cvs(CV["b1"])[:, g:g + 1],
                        OP.mult, OP.add)
                return inst
            p.op("vector", _gb, reads=[dd, cv], writes=[hc])
            if i == 0:
                p.op("vector", lambda e: e.memset(hc.ap()[:, :, 0:1], 0.0),
                     reads=[hc], writes=[hc])
            elif i == NP:
                p.op("vector", lambda e: e.tensor_mul(
                    hc.ap()[:, :, 0:1], hp.ap()[:, :, TT:TT + 1],
                    bg_t.ap().unsqueeze(2)),
                    reads=[hp, bg_t, hc], writes=[hc])
            else:
                p.op("vector", lambda e: e.tensor_copy(
                    hc.ap()[:, :, 0:1], hp.ap()[:, :, TT:TT + 1]),
                    reads=[hp, hc], writes=[hc])
            if not pre:
                dump("h", hc, hc.ap()[:, :, 1:TT + 1], t0, dense=False)
            p.op("gpsimd", lambda e: e.tensor_sub(
                dd.ap(), hc.ap()[:, :, 1:TT + 1], hc.ap()[:, :, 0:TT]),
                reads=[hc], writes=[dd])
            mixes = [(xkb, "tmk"), (xvb, "tmv")] + ([(xrb, "tmr")] if full else [])
            for dst, nm in mixes:
                def _mix(e, dst=dst, nm=nm):
                    for g in range(G):
                        inst = e.scalar_tensor_tensor(
                            dst.ap()[:, g, :], dd.ap()[:, g, :], cvs(CV[nm])[:, g:g + 1],
                            hc.ap()[:, g, 0:TT], OP.mult, OP.add)
                    return inst
                p.op("vector", _mix, reads=[dd, hc, cv], writes=[dst])
            grp = [0]
            def mm_pair(w_sb, src_b, dg, dst, ev_fn):
                ps = p4[grp[0] % 4]
                grp[0] += 1
                def _mm(e, ps=ps, w=w_sb, s=src_b, dg=dg):
                    for cg in range(G):
                        inst = e.matmul(ps.ap(), w.ap()[:, cg, dg * 128:(dg + 1) * 128],
                                        s.ap()[:, cg, :], start=(cg == 0), stop=(cg == G - 1))
                    return inst
                p.op("tensor", _mm, reads=[w_sb, src_b], writes=[ps])
                ev_fn(dst, ps, dg)

            def ev_dve(dst, ps, dg):
                p.op("scalar", lambda e, d=dst, ps=ps, dg=dg: e.activation(
                    d.ap()[:, dg, :], ps.ap(), AF.Copy), reads=[ps], writes=[dst])

            def ev_exp(dst, ps, dg):
                p.op("scalar", lambda e, d=dst, ps=ps, dg=dg: e.activation(
                    d.ap()[:, dg, :], ps.ap(), AF.Exp, scale=-1.0),
                    reads=[ps], writes=[dst])

            for dg in range(G):
                mm_pair(wk_sb, xkb, dg, kt, ev_dve)
            if full:
                p.op("gpsimd", lambda e, hin=hin: e.tensor_mul(hin.ap(), x0.ap(), bcast_g(r_r0)),
                     reads=[x0, r_r0], writes=[hin])
                p.op("gpsimd", lambda e, hin=hin: e.tensor_sub(hin.ap(), hin.ap(), bcast_g(r_mr0)),
                     reads=[hin, r_mr0], writes=[hin])
            for dg in range(G):
                mm_pair(wv_sb, xvb, dg, vt, ev_dve)
            if full:
                for dg in range(G):
                    mm_pair(wr_sb, xrb, dg, sr, ev_exp)
                p.op("scalar", lambda e, sr=sr: e.activation(sr.ap(), sr.ap(), AF.Ln, bias=1.0),
                     reads=[sr, c_one], writes=[sr])
                p.op("scalar", lambda e, sr=sr: e.activation(sr.ap(), sr.ap(), AF.Exp, scale=-1.0),
                     reads=[sr], writes=[sr])

        def ph1_back(i, pre, full, t0, s2_next=None, s1_next2=None):
            kt, vt, sr, hin = ktd[i % 2], vtd[i % 2], srd[i % 2], hind[i % 2]
            x1t = kt
            p.op("vector", lambda e, kt=kt: e.tensor_reduce(
                rmax.ap().unsqueeze(2), kt.ap(), mybir.AxisListType.X, OP.max),
                reads=[kt], writes=[rmax])
            p.op("vector", lambda e: e.tensor_max(mnew.ap(), m_st.ap(), rmax.ap()),
                 reads=[m_st, rmax], writes=[mnew])
            p.op("vector", lambda e: e.tensor_sub(sdel.ap(), m_st.ap(), mnew.ap()),
                 reads=[m_st, mnew], writes=[sdel])
            p.op("scalar", lambda e: e.activation(srsc.ap(), sdel.ap(), AF.Exp),
                 reads=[sdel], writes=[srsc])
            p.op("vector", lambda e: e.tensor_mul(a_st.ap(), a_st.ap(), srsc.ap()),
                 reads=[a_st, srsc], writes=[a_st])
            p.op("vector", lambda e: e.tensor_mul(b_st.ap(), b_st.ap(), srsc.ap()),
                 reads=[b_st, srsc], writes=[b_st])
            p.op("vector", lambda e: e.tensor_copy(m_st.ap(), mnew.ap()),
                 reads=[mnew], writes=[m_st])
            gb = gate_t if pre else zero8
            p.op("vector", lambda e, gb=gb: e.tensor_sub(ebias.ap(), gb.ap(), mnew.ap()),
                 reads=[gb, mnew], writes=[ebias])
            if full:
                p.op("vector", lambda e: e.tensor_sub(ubias.ap(), cvs(CV["u"]), mnew.ap()),
                     reads=[cv, mnew], writes=[ubias])

            def _ekx(e, kt=kt):
                for g in range(G):
                    inst = e.activation(ek.ap()[:, g, :], kt.ap()[:, g, :], AF.Exp,
                                        bias=ebias.ap()[:, g:g + 1])
                return inst
            p.op("scalar", _ekx, reads=[kt, ebias], writes=[ek])
            if full:
                def _ukx(e, kt=kt):
                    for g in range(G):
                        inst = e.activation(kt.ap()[:, g, :], kt.ap()[:, g, :], AF.Exp,
                                            bias=ubias.ap()[:, g:g + 1])
                    return inst
                p.op("scalar", _ukx, reads=[kt, ubias], writes=[kt])
            if s2_next is not None:
                s2_next()
            p.op("vector", lambda e, vt=vt: e.tensor_mul(ekv.ap(), ek.ap(), vt.ap()),
                 reads=[ek, vt], writes=[ekv])
            def _scanA(e):
                for g in range(G):
                    inst = e.tensor_tensor_scan(
                        ekv.ap()[:, g, :],
                        lam.ap()[:, g:g + 1].broadcast_to([128, TT]),
                        ekv.ap()[:, g, :], a_st.ap()[:, g:g + 1], OP.mult, OP.add)
                return inst
            p.op("vector", _scanA, reads=[ekv, lam, a_st], writes=[ekv])
            def _scanB(e):
                for g in range(G):
                    inst = e.tensor_tensor_scan(
                        ek.ap()[:, g, :],
                        lam.ap()[:, g:g + 1].broadcast_to([128, TT]),
                        ek.ap()[:, g, :], b_st.ap()[:, g:g + 1], OP.mult, OP.add)
                return inst
            p.op("vector", _scanB, reads=[ek, lam, b_st], writes=[ek])
            p.op("vector", lambda e: e.drain(), reads=[ekv, ek], writes=[])
            if not full:
                p.op("vector", lambda e: e.tensor_copy(a_st.ap().unsqueeze(2), ekv.ap()[:, :, TT - 1:TT]),
                     reads=[ekv], writes=[a_st])
                p.op("vector", lambda e: e.tensor_copy(b_st.ap().unsqueeze(2), ek.ap()[:, :, TT - 1:TT]),
                     reads=[ek], writes=[b_st])
                if s1_next2 is not None:
                    s1_next2()
                return
            p.op("gpsimd", lambda e, vt=vt, kt=kt: e.tensor_mul(vt.ap(), vt.ap(), kt.ap()),
                 reads=[vt, kt], writes=[vt])
            p.op("gpsimd", lambda e, vt=vt: e.tensor_add(
                vt.ap()[:, :, 1:TT], vt.ap()[:, :, 1:TT], ekv.ap()[:, :, 0:TT - 1]),
                reads=[vt, ekv], writes=[vt])
            p.op("gpsimd", lambda e, vt=vt: e.tensor_add(
                vt.ap()[:, :, 0:1], vt.ap()[:, :, 0:1], a_st.ap().unsqueeze(2)),
                reads=[vt, a_st], writes=[vt])
            p.op("vector", lambda e, kt=kt: e.tensor_add(
                ekv.ap()[:, :, 0:1], kt.ap()[:, :, 0:1], b_st.ap().unsqueeze(2)),
                reads=[kt, b_st, ekv], writes=[ekv])
            p.op("vector", lambda e: e.tensor_copy(a_st.ap().unsqueeze(2), ekv.ap()[:, :, TT - 1:TT]),
                 reads=[ekv], writes=[a_st])
            p.op("vector", lambda e: e.tensor_copy(b_st.ap().unsqueeze(2), ek.ap()[:, :, TT - 1:TT]),
                 reads=[ek], writes=[b_st])
            p.op("vector", lambda e, kt=kt: e.tensor_add(
                ekv.ap()[:, :, 1:TT], kt.ap()[:, :, 1:TT], ek.ap()[:, :, 0:TT - 1]),
                reads=[kt, ek, ekv], writes=[ekv])
            if not pre:
                dump("npre", vt, vt.ap(), t0)
                dump("ek2", kt, kt.ap(), t0)
            p.op("scalar", lambda e: e.activation(ekv.ap(), ekv.ap(), AF.Ln),
                 reads=[ekv], writes=[ekv])
            p.op("scalar", lambda e: e.activation(ekv.ap(), ekv.ap(), AF.Exp, scale=-1.0),
                 reads=[ekv], writes=[ekv])
            if not pre:
                dump("idn", ekv, ekv.ap(), t0)
            p.op("gpsimd", lambda e, vt=vt: e.tensor_mul(vt.ap(), vt.ap(), ekv.ap()),
                 reads=[vt, ekv], writes=[vt])
            if not pre:
                dump("wkv", vt, vt.ap(), t0)
            p.op("gpsimd", lambda e, vt=vt, sr=sr: e.tensor_mul(rwkb.ap(), vt.ap(), sr.ap()),
                 reads=[vt, sr], writes=[rwkb])
            if not pre:
                dump("rwkv", rwkb, rwkb.ap(), t0)
            for cg in range(G):
                ps = ps_m0 if cg % 2 == 0 else ps_m1
                def _omm(e, ps=ps, cg=cg):
                    for dg in range(G):
                        inst = e.matmul(ps.ap(), wo_sb.ap()[:, dg, cg * 128:(cg + 1) * 128],
                                        rwkb.ap()[:, dg, :], start=(dg == 0), stop=(dg == G - 1))
                    return inst
                p.op("tensor", _omm, reads=[wo_sb, rwkb], writes=[ps])
                p.op("vector", lambda e, ps=ps, cg=cg, x1t=x1t, hin=hin: e.tensor_add(
                    x1t.ap()[:, cg, :], ps.ap(), hin.ap()[:, cg, :]),
                    reads=[ps, hin, x1t], writes=[x1t])
            if pre:
                def _bd(e, h3=x1_s3, x1t=x1t):
                    with self.nc.allow_non_contiguous_dma(reason="4KB boundary col"):
                        return e.dma_start(out=h3[:, :, 0:1], in_=x1t.ap()[:, :, TT - 1:TT])
                p.op("gpsimd", _bd, reads=[x1t], dma=True)
            else:
                p.op("gpsimd", lambda e, t=t0, h3=x1_s3, x1t=x1t: e.dma_start(
                    out=h3[:, :, 1 + t:1 + t + TT], in_=x1t.ap()),
                    reads=[x1t], dma=True)
            if s1_next2 is not None:
                s1_next2()

        wq_acc = self.sbuf("wq_acc", (128, 192))
        _wqs = []
        for hb_, nm_ in ((hind[0], "A"), (hind[1], "B")):
            fl = hb_.ap().rearrange("p g t -> p (g t)")
            _wqs += [Buf(fl[:, j * 512:(j + 1) * 512], f"wqs{nm_}{j}")
                     for j in range(4)]

        class WQSched:

            def __init__(self, w_h, nrow_groups, base):
                ncols = 4096 if nrow_groups == G else 1024
                wv3 = w_h.rearrange("(g p) d -> p g d", p=128)
                self.d, self.r = [], []
                idx = 0
                for gch in range(nrow_groups):
                    for s in range(ncols // 512):
                        def dm(gch=gch, s=s, idx=idx, wv3=wv3):
                            hb = _wqs[idx % 8]
                            p.op("scalar", lambda e, hb=hb: e.dma_start(
                                out=hb.ap(), in_=wv3[:, gch, s * 512:(s + 1) * 512]),
                                writes=[hb], dma=True)
                        def rd(idx=idx, base=base):
                            hb = _wqs[idx % 8]
                            p.op("scalar", lambda e, hb=hb, base=base, idx=idx: e.activation(
                                hb.ap(), hb.ap(), AF.Abs,
                                accum_out=wq_acc.ap()[:, base + idx:base + idx + 1]),
                                reads=[hb], writes=[hb, wq_acc])
                        self.d.append(dm)
                        self.r.append(rd)
                        idx += 1
                self.di = self.ri = 0

            def step(self, n):
                for _ in range(n):
                    if self.di - self.ri >= 8 or self.di >= len(self.d):
                        if self.ri < self.di:
                            self.r[self.ri]()
                            self.ri += 1
                    if self.di < len(self.d):
                        self.d[self.di]()
                        self.di += 1

            def flush(self):
                while self.ri < self.di:
                    self.r[self.ri]()
                    self.ri += 1

        wq_g = WQSched(wgT, G, 0)
        wq_u = WQSched(wuT, G, 64)
        wq_d = WQSched(wdT, FG, 128)
        self._wq1 = {"u": wq_u, "d": wq_d}

        def ph1_args(j):
            return (j, j < NP, (j >= NP) or (j == NP - 1),
                    x_pre if j < NP else x_own,
                    (j if j < NP else j - NP) * TT,
                    h[j % 2], h[(j + 1) % 2])

        NTOT = NP + NT
        ph1_s1(*ph1_args(0))
        ph1_s2(*ph1_args(0))
        if NTOT > 1:
            ph1_s1(*ph1_args(1))
        for i in range(NTOT):
            s2n = (lambda j=i + 1: ph1_s2(*ph1_args(j))) if i + 1 < NTOT else None
            s1n = (lambda j=i + 2: ph1_s1(*ph1_args(j))) if i + 2 < NTOT else None
            a = ph1_args(i)
            ph1_back(a[0], a[1], a[2], a[4], s2_next=s2n, s1_next2=s1n)
            if i < 6:
                wq_g.step(11)
            if i == 5:
                wq_g.flush()
                p.op("vector", lambda e: e.tensor_copy(
                    t8.ap()[:, 0:1], wq_acc.ap()[:, 0:1]),
                    reads=_wqs + [wq_acc], writes=[hind[0], hind[1]])

        if "x1" in self.debug_dump:
            dbg = self.dram_out("dbg_x1", (C, TOWN + 1))
            p.barrier()
            p.op("sync", lambda e: e.dma_start(out=dbg, in_=x1_s[:, :]), dma=True)
        if stop_after == "ph1":
            self._finish(p, sems)
            return nc

        p.barrier()
        x1v = x1_s3
        xk2_s3 = xk2_s.rearrange("(g p) t -> p g t", p=128)
        sr2_s3 = sr2_s.rearrange("(g p) t -> p g t", p=128)
        out1_s3 = out1_s.rearrange("(g p) t -> p g t", p=128)
        xhat1_s3 = xhat1_s.rearrange("(g p) t -> p g t", p=128)
        gpre_s3 = gpre_s.rearrange("(q p) t -> p q t", p=128)
        kk_s3 = kk_s.rearrange("(q p) t -> p q t", p=128)
        xq2_s3 = xq2_s.rearrange("(q p) t -> p q t", p=128)
        ones_f1 = self.sbuf("ones_f1", (1, 128))
        p.op("vector", lambda e: e.memset(ones_f1.ap(), 1.0), writes=[ones_f1])
        h2bd = self.sbuf("h2bd", (128, G))
        wfr_sb = Buf(ware.ap()[:, :, 0:C], "wfr", track=ware)
        p.op("gpsimd", lambda e: e.dma_start(
            out=wfr_sb.ap(), in_=wfrT.rearrange("(g p) d -> p g d", p=128)),
            writes=[ware], dma=True)

        def stats_n(src_b, sq_b, n, ps1, ps2):
            def f(e, a=src_b, b=sq_b, n=n, ps1=ps1, ps2=ps2):
                for q, ps in ((a, ps1), (b, ps2)):
                    for g in range(G):
                        inst = e.matmul(ps.ap()[:, 0:n], ones_bf.ap(), q.ap()[:, g, 0:n],
                                        start=(g == 0), stop=(g == G - 1))
                return inst
            p.op("tensor", f, reads=[src_b, sq_b, ones_bf], writes=[ps1, ps2])

        def rows_plain(n, r_out, mr_out):
            p.op("scalar", lambda e: e.activation(r_msq.ap()[:, 0:n], ps_S.ap()[:, 0:n], AF.Square, scale=1.0 / C),
                 reads=[ps_S], writes=[r_msq])
            p.op("vector", lambda e: e.scalar_tensor_tensor(
                r_v0.ap()[:, 0:n], ps_Q.ap()[:, 0:n], 1.0 / C, r_msq.ap()[:, 0:n], OP.mult, OP.subtract),
                reads=[ps_Q, r_msq], writes=[r_v0])
            p.op("scalar", lambda e: e.activation(r_L.ap()[:, 0:n], r_v0.ap()[:, 0:n], AF.Ln, bias=EPS),
                 reads=[r_v0, c_eps], writes=[r_L])
            p.op("scalar", lambda e: e.activation(r_out.ap()[:, 0:n], r_L.ap()[:, 0:n], AF.Exp, scale=-0.5),
                 reads=[r_L], writes=[r_out])
            p.op("vector", lambda e: e.scalar_tensor_tensor(
                mr_out.ap()[:, 0:n], ps_S.ap()[:, 0:n], 1.0 / C, r_out.ap()[:, 0:n], OP.mult, OP.mult),
                reads=[ps_S, r_out], writes=[mr_out])

        x1bd = ekv
        x1bdb = x0b
        def _ldbd(e):
            with self.nc.allow_non_contiguous_dma(reason="4KB boundary col"):
                return e.dma_start(out=x1bd.ap()[:, :, 0:1], in_=x1v[:, :, 0:1])
        p.op("sync", _ldbd, writes=[x1bd], dma=True)
        p.op("gpsimd", lambda e: e.tensor_copy(x1bdb.ap()[:, :, 0:1], x1bd.ap()[:, :, 0:1]),
             reads=[x1bd], writes=[x1bdb])
        p.op("scalar", lambda e: e.activation(xsqb.ap()[:, :, 0:1], x1bd.ap()[:, :, 0:1], AF.Square),
             reads=[x1bd], writes=[xsqb])
        stats_n(x1bdb, xsqb, 1, ps_S, ps_Q)
        rows_plain(1, r_rho, r_mrho)
        def _bd1(e):
            for g in range(G):
                inst = e.tensor_scalar(
                    dd.ap()[:, g, 0:1], x1bd.ap()[:, g, 0:1], r_rho.ap()[:, 0:1],
                    r_mrho.ap()[:, 0:1], OP.mult, OP.subtract)
            return inst
        p.op("vector", _bd1, reads=[x1bd, r_rho, r_mrho], writes=[dd])
        def _bd2(e):
            for g in range(G):
                inst = e.tensor_scalar(
                    h2bd.ap()[:, g:g + 1], dd.ap()[:, g, 0:1],
                    cvs(CV["g2"])[:, g:g + 1], cvs(CV["b2"])[:, g:g + 1], OP.mult, OP.add)
            return inst
        p.op("vector", _bd2, reads=[dd, cv], writes=[h2bd])
        p.op("vector", lambda e: e.tensor_mul(h2bd.ap(), h2bd.ap(), bg_t.ap()),
             reads=[h2bd, bg_t], writes=[h2bd])

        def ph2a_tile(i, t0, hc, hp):
            p.op("sync", lambda e: e.dma_start(
                out=x0.ap(), in_=x1v[:, :, 1 + t0:1 + t0 + TT]), writes=[x0], dma=True)
            p.op("gpsimd", lambda e: e.tensor_copy(x0b.ap(), x0.ap()), reads=[x0], writes=[x0b])
            p.op("scalar", lambda e: e.activation(xsqb.ap(), x0.ap(), AF.Square), reads=[x0], writes=[xsqb])
            stats_n(x0b, xsqb, TT, ps_S, ps_Q)
            rows_plain(TT, r_rho, r_mrho)
            p.op("gpsimd", lambda e: e.tensor_mul(dd.ap(), x0.ap(), bcast_g(r_rho)),
                 reads=[x0, r_rho], writes=[dd])
            p.op("gpsimd", lambda e: e.tensor_sub(dd.ap(), dd.ap(), bcast_g(r_mrho)),
                 reads=[dd, r_mrho], writes=[dd])
            def _gb2(e):
                for g in range(G):
                    inst = e.tensor_scalar(
                        hc.ap()[:, g, 1:TT + 1], dd.ap()[:, g, :],
                        cvs(CV["g2"])[:, g:g + 1], cvs(CV["b2"])[:, g:g + 1], OP.mult, OP.add)
                return inst
            p.op("vector", _gb2, reads=[dd, cv], writes=[hc])
            if i == 0:
                p.op("vector", lambda e: e.tensor_copy(hc.ap()[:, :, 0:1], h2bd.ap().unsqueeze(2)),
                     reads=[h2bd, hc], writes=[hc])
            else:
                p.op("vector", lambda e: e.tensor_copy(hc.ap()[:, :, 0:1], hp.ap()[:, :, TT:TT + 1]),
                     reads=[hp, hc], writes=[hc])
            p.op("vector", lambda e: e.tensor_sub(
                dd.ap(), hc.ap()[:, :, 1:TT + 1], hc.ap()[:, :, 0:TT]), reads=[hc], writes=[dd])
            for dst, nm in ((xvb, "ftmk"), (xrb, "ftmr")):
                def _mix2(e, dst=dst, nm=nm):
                    for g in range(G):
                        inst = e.scalar_tensor_tensor(
                            dst.ap()[:, g, :], dd.ap()[:, g, :], cvs(CV[nm])[:, g:g + 1],
                            hc.ap()[:, g, 0:TT], OP.mult, OP.add)
                    return inst
                p.op("vector", _mix2, reads=[dd, hc, cv], writes=[dst])
            p.op("scalar", lambda e: e.dma_start(out=xk2_s3[:, :, t0:t0 + TT], in_=xvb.ap()),
                 reads=[xvb], dma=True)
            for dg in range(G):
                ps = ps_m0 if dg % 2 == 0 else ps_m1
                def _rmm(e, ps=ps, dg=dg):
                    for cg in range(G):
                        inst = e.matmul(ps.ap(), wfr_sb.ap()[:, cg, dg * 128:(dg + 1) * 128],
                                        xrb.ap()[:, cg, :], start=(cg == 0), stop=(cg == G - 1))
                    return inst
                p.op("tensor", _rmm, reads=[wfr_sb, xrb], writes=[ps])
                p.op("scalar", lambda e, ps=ps, dg=dg: e.activation(
                    ek.ap()[:, dg, :], ps.ap(), AF.Exp, scale=-1.0), reads=[ps], writes=[ek])
            p.op("scalar", lambda e: e.activation(sr.ap(), ek.ap(), AF.Ln, bias=1.0),
                 reads=[ek, c_one], writes=[sr])
            sb32 = Buf(big32.ap()[:, 0:G, :], "sr2b", track=big32)
            p.op("scalar", lambda e, d=sb32: e.activation(d.ap(), sr.ap(), AF.Exp, scale=-1.0),
                 reads=[sr], writes=[big32])
            p.op("scalar", lambda e, s=sb32: e.dma_start(out=sr2_s3[:, :, t0:t0 + TT], in_=s.ap()),
                 reads=[big32], dma=True)

        for i in range(NT):
            ph2a_tile(i, i * TT, h[i % 2], h[(i + 1) % 2])
            self._wq1["u"].step(4)

        if "xk2" in self.debug_dump:
            o1_ = self.dram_out("dbg_xk2", (C, TOWN), BF16)
            p.barrier()
            p.op("sync", lambda e, o=o1_: e.dma_start(out=o, in_=xk2_s[:, :]), dma=True)
        if "sr2" in self.debug_dump:
            o2_ = self.dram_out("dbg_sr2", (C, TOWN), BF16)
            p.barrier()
            p.op("sync", lambda e, o=o2_: e.dma_start(out=o, in_=sr2_s[:, :]), dma=True)
        if stop_after == "ph2a":
            self._finish(p, sems)
            return nc

        p.barrier()
        _wfk3 = wfkT.rearrange("(g p) d -> p g d", p=128)
        for _q in range(4):
            p.op("gpsimd", lambda e, q=_q: e.dma_start(
                out=ware.ap()[:, :, q * 1024:(q + 1) * 1024],
                in_=_wfk3[:, :, q * 1024:(q + 1) * 1024]),
                writes=[ware], dma=True)

        rot6 = p4 + [ps_m0, ps_m1]
        rot5 = p4 + [ps_m0]
        rl_sl = [Buf(ktd[0].ap()[:, j, :], f"rl{j}") for j in range(4)]
        st_ch = [xsqb, x0b]
        st_sl = [[Buf(st_ch[c].ap()[:, j, :], f"st{c}_{j}") for j in range(G)]
                 for c in range(2)]

        def ph2b1_tile(i, t0, xk2t):
            p.op("sync", lambda e, x=xk2t: e.dma_start(
                out=x.ap(), in_=xk2_s3[:, :, t0:t0 + TT]), writes=[xk2t], dma=True)
            for fg in range(FG):
                ps = rot6[fg % 6]
                def _kmm(e, ps=ps, fg=fg, x=xk2t):
                    for cg in range(G):
                        inst = e.matmul(ps.ap(), ware.ap()[:, cg, fg * 128:(fg + 1) * 128],
                                        x.ap()[:, cg, :], start=(cg == 0), stop=(cg == G - 1))
                    return inst
                p.op("tensor", _kmm, reads=[ware, xk2t], writes=[ps])
                rl = rl_sl[fg % 4]
                p.op("scalar", lambda e, ps=ps, rl=rl: e.activation(
                    rl.ap(), ps.ap(), AF.Relu), reads=[ps], writes=[rl])
                ch = (fg // G) % 2
                dst = st_sl[ch][fg % G]
                p.op("vector" if ch == 0 else "gpsimd",
                     lambda e, rl=rl, dst=dst: e.tensor_mul(
                         dst.ap(), rl.ap(), rl.ap()), reads=[rl], writes=[dst, st_ch[ch]])
                if fg % G == G - 1:
                    p.op("scalar", lambda e, fg=fg, ch=ch: e.dma_start(
                        out=kk_s3[:, fg - G + 1:fg + 1, t0:t0 + TT], in_=st_ch[ch].ap()),
                        reads=st_sl[ch] + [st_ch[ch]], dma=True)

        for i in range(NT):
            ph2b1_tile(i, i * TT, xvb if i % 2 == 0 else xrb)
            self._wq1["u"].step(5)
        if "kk" in self.debug_dump:
            o4_ = self.dram_out("dbg_kk", (F, TOWN), BF16)
            p.barrier()
            for _c in range(8):
                p.op("sync", lambda e, o=o4_, c=_c: e.dma_start(
                    out=o[c * 512:(c + 1) * 512, :], in_=kk_s[c * 512:(c + 1) * 512, :]), dma=True)

        p.barrier()
        wfv32 = Buf(ware.ap().rearrange("p g (x d) -> p (g x) d", x=4), "wfv32", track=ware)
        _wfv3 = wfvT.rearrange("(q p) d -> p q d", p=128)
        for _q in range(4):
            p.op("gpsimd", lambda e, q=_q: e.dma_start(
                out=wfv32.ap()[:, q * 8:(q + 1) * 8, :],
                in_=_wfv3[:, q * 8:(q + 1) * 8, :]),
                writes=[ware], dma=True)
        if "kv" in self.debug_dump:
            dbg_kv = self.dram_out("dbg_kv", (C, TOWN)).rearrange("(g p) t -> p g t", p=128)
        if "x2" in self.debug_dump:
            dbg_x2 = self.dram_out("dbg_x2", (C, TOWN)).rearrange("(g p) t -> p g t", p=128)

        def ph2b2_tile(i, t0):
            x1t_ = kt; x2t = vt; x0t = ek
            sr2t = x0b
            x2sq = xsqb
            p.op("sync", lambda e: e.dma_start(out=x1t_.ap(), in_=x1v[:, :, 1 + t0:1 + t0 + TT]),
                 writes=[x1t_], dma=True)
            p.op("sync", lambda e, s=sr2t: e.dma_start(out=s.ap(), in_=sr2_s3[:, :, t0:t0 + TT]),
                 writes=[sr2t], dma=True)
            xsrc = x_own.rearrange("(g p) t -> p g t", p=128)
            p.op("sync", lambda e, xs=xsrc: e.dma_start(out=x0t.ap(), in_=xs[:, :, t0:t0 + TT]),
                 writes=[x0t], dma=True)
            kkt = Buf(big32.ap()[:, 0:32, :], "kkt", track=big32)
            p.op("sync", lambda e, k=kkt: e.dma_start(
                out=k.ap(), in_=kk_s3[:, :, t0:t0 + TT]), writes=[big32], dma=True)
            for cg in range(G):
                ps = rot6[cg % 6]
                def _vmm(e, cg=cg, ps=ps, k=kkt):
                    for fg in range(FG):
                        inst = e.matmul(ps.ap(), wfv32.ap()[:, fg, cg * 128:(cg + 1) * 128],
                                        k.ap()[:, fg, :], start=(fg == 0), stop=(fg == FG - 1))
                    return inst
                p.op("tensor", _vmm, reads=[wfv32, big32], writes=[ps])
                if "kv" in self.debug_dump:
                    p.op("vector", lambda e, cg=cg, ps=ps: e.tensor_copy(
                        hin.ap()[:, cg, :], ps.ap()), reads=[ps, hin], writes=[hin])
                p.op("vector", lambda e, s=sr2t, cg=cg, ps=ps: e.tensor_mul(
                    x2t.ap()[:, cg, :], ps.ap(), s.ap()[:, cg, :]),
                    reads=[ps, sr2t, x2t], writes=[x2t])
            if "kv" in self.debug_dump:
                p.op("scalar", lambda e, t=t0: e.dma_start(
                    out=dbg_kv[:, :, t:t + TT], in_=hin.ap()), reads=[hin], dma=True)
            p.op("gpsimd", lambda e: e.tensor_add(x2t.ap(), x2t.ap(), x1t_.ap()),
                 reads=[x2t, x1t_], writes=[x2t])
            if "x2" in self.debug_dump:
                p.op("scalar", lambda e, t=t0: e.dma_start(
                    out=dbg_x2[:, :, t:t + TT], in_=x2t.ap()), reads=[x2t], dma=True)
            x2b = Buf(big32.ap()[:, 0:G, :], "x2b", track=big32)
            p.op("gpsimd", lambda e, d=x2b: e.tensor_copy(d.ap(), x2t.ap()), reads=[x2t, big32], writes=[big32])
            p.op("scalar", lambda e, d=x2sq: e.activation(d.ap(), x2t.ap(), AF.Square),
                 reads=[x2t], writes=[x2sq])
            stats_n(x2b, x2sq, TT, ps_S, ps_Q)
            rows_plain(TT, r_rho, r_mrho)
            p.op("gpsimd", lambda e: e.tensor_mul(dd.ap(), x2t.ap(), bcast_g(r_rho)),
                 reads=[x2t, r_rho], writes=[dd])
            p.op("gpsimd", lambda e: e.tensor_sub(dd.ap(), dd.ap(), bcast_g(r_mrho)),
                 reads=[dd, r_mrho], writes=[dd])
            p.op("gpsimd", lambda e: e.tensor_add(dd.ap(), dd.ap(), x0t.ap()),
                 reads=[dd, x0t], writes=[dd])
            p.op("scalar", lambda e: e.dma_start(out=out1_s3[:, :, t0:t0 + TT], in_=dd.ap()),
                 reads=[dd], dma=True)

        for i in range(NT):
            ph2b2_tile(i, i * TT)
            self._wq1["d"].step(4)

        if "out1" in self.debug_dump:
            o3_ = self.dram_out("dbg_out1", (C, TOWN))
            p.barrier()
            p.op("sync", lambda e, o=o3_: e.dma_start(out=o, in_=out1_s[:, :]), dma=True)
        if stop_after == "ph2":
            self._finish(p, sems)
            return nc

        mscal = {}

        def wquant(w_h, nrow_groups, name, base):
            nchunk = nrow_groups
            ncols = 4096 if nrow_groups == G else 1024
            cw = min(2048, ncols)
            wview3 = w_h.rearrange("(g p) d -> p g d", p=128)
            stg = [(Buf(x0.ap().rearrange("p g t -> p (g t)"), "sflat", track=x0), x0),
                   (Buf(dd.ap().rearrange("p g t -> p (g t)"), "sflat2", track=dd), dd)]
            QR = ["sync", "gpsimd", "scalar"]
            nsub = ncols // cw
            p.op("vector", lambda e: e.tensor_reduce(
                rmax.ap()[:, 0:1], wq_acc.ap()[:, base:base + 64], mybir.AxisListType.X, OP.add),
                reads=[wq_acc], writes=[rmax])
            if "wq_row" not in self.sb:
                self.sb["wq_row"] = self.sbuf("wq_row", (1, 128))
                self.sb["wq_thr"] = self.sbuf("wq_thr", (128, 2))
            rowt = self.sb["wq_row"]
            def _rowt(e):
                with self.nc.allow_non_contiguous_dma(reason="128x4B gather"):
                    return e.dma_start(out=rowt.ap(), in_=rmax.ap()[:, 0:1])
            p.op("gpsimd", _rowt, reads=[rmax], writes=[rowt], dma=True)
            msc = self.sbuf(f"wq_m_{name}", (1, 1))
            p.op("vector", lambda e: e.tensor_reduce(
                msc.ap(), rowt.ap(), mybir.AxisListType.X, OP.add), reads=[rowt], writes=[msc])
            cnt = float(nrow_groups * 128 * ncols)
            p.op("vector", lambda e, c=cnt: e.tensor_scalar(
                msc.ap(), msc.ap(), 1.0 / c, 1e-5, OP.mult, OP.max), reads=[msc], writes=[msc])
            mp = self.sbuf(f"wq_mp_{name}", (128, 1))
            p.op("tensor", lambda e: e.matmul(ps_m0.ap()[:, 0:1], ones_f1.ap(), msc.ap(),
                                              start=True, stop=True),
                 reads=[ones_f1, msc], writes=[ps_m0])
            p.op("vector", lambda e: e.tensor_copy(mp.ap(), ps_m0.ap()[:, 0:1]),
                 reads=[ps_m0], writes=[mp])
            thr = self.sb["wq_thr"]
            p.op("vector", lambda e: e.tensor_scalar_mul(thr.ap()[:, 0:1], mp.ap(), 0.5),
                 reads=[mp], writes=[thr])
            p.op("vector", lambda e: e.tensor_scalar_mul(thr.ap()[:, 1:2], thr.ap()[:, 0:1], -1.0),
                 reads=[thr], writes=[thr])
            wareflat = Buf(ware.ap().rearrange("p g d -> p (g d)"), "wareflat", track=ware)
            idx = 0
            for gch in range(nchunk):
                for s in range(nsub):
                    sf, sb = stg[idx % 2]
                    p.op(QR[idx % 3], lambda e, gch=gch, s=s, cw=cw, sf=sf: e.dma_start(
                        out=sf.ap()[:, 0:cw], in_=wview3[:, gch, s * cw:(s + 1) * cw]),
                        writes=[sb], dma=True)
                    off = (gch * ncols + s * cw)
                    p.op("vector", lambda e, off=off, cw=cw, sf=sf, sb=sb: e._custom_dve(
                        self.ops["ATH_TERN"], out=wareflat.ap()[:, off:off + cw],
                        in0=sf.ap()[:, 0:cw], s0=thr.ap()[:, 0:1], s1=thr.ap()[:, 1:2]),
                        reads=[sb, thr], writes=[ware])
                    idx += 1
            mscal[name] = (mp, msc)
            return mp, msc

        p.barrier()
        wq_g.flush()
        mg_p, mg_s = wquant(wgT, G, "g", 0)
        amax_tok = self.sbuf("amax_tok", (128, 8))
        p.op("vector", lambda e: e.memset(amax_tok.ap(), 0.0), writes=[amax_tok])
        rows8 = self.sbuf("rows8", (8, 128))
        row1 = self.sbuf("row1", (1, 2 * TT))
        ps_bc = ps_S
        ps_tp = ps_m1

        def amax_machinery(Mb, pre_ops, deq_ops, qrow_t0=None):
            for c in range(2):
                p.op("tensor", lambda e, c=c: e.transpose(
                    ps_tp.ap()[:, 0:128], Mb.ap()[:, c * 128:(c + 1) * 128], ident.ap()),
                    reads=[Mb, ident], writes=[ps_tp])
                p.op("vector", lambda e, c=c: e.tensor_reduce(
                    amax_tok.ap()[:, c:c + 1], ps_tp.ap()[:, 0:128], mybir.AxisListType.X, OP.max),
                    reads=[ps_tp], writes=[amax_tok])
            for f in pre_ops:
                p.op("vector", f, reads=[amax_tok, mg_p, mu_p] if mu_p is not None else [amax_tok],
                     writes=[amax_tok])
            p.op("vector", lambda e: e.reciprocal(
                amax_tok.ap()[:, 4:6], amax_tok.ap()[:, 0:2]), reads=[amax_tok], writes=[amax_tok])
            p.op("vector", lambda e: e.drain(), reads=[amax_tok])
            for f in deq_ops:
                p.op("vector", f, reads=[amax_tok, mg_p, mu_p] if mu_p is not None else [amax_tok],
                     writes=[amax_tok])
            p.op("tensor", lambda e: e.transpose(
                ps_tp.ap()[0:8, 0:128], amax_tok.ap()[:, 0:8], ident.ap()),
                reads=[amax_tok, ident], writes=[ps_tp])
            p.op("vector", lambda e: e.tensor_copy(rows8.ap(), ps_tp.ap()[0:8, 0:128]),
                 reads=[ps_tp], writes=[rows8])
            def _row_inv(e):
                with self.nc.allow_non_contiguous_dma(reason="row gather"):
                    return e.dma_start(out=row1.ap()[:, 0:TT], in_=rows8.ap()[4:6, :])
            p.op("gpsimd", _row_inv, reads=[rows8], writes=[row1], dma=True)
            def _row_deq(e):
                with self.nc.allow_non_contiguous_dma(reason="row gather"):
                    return e.dma_start(out=row1.ap()[:, TT:2 * TT], in_=rows8.ap()[2:4, :])
            p.op("gpsimd", _row_deq, reads=[rows8], writes=[row1], dma=True)
            if qrow_t0 is not None:
                def _row_q(e, t0=qrow_t0):
                    with self.nc.allow_non_contiguous_dma(reason="row store"):
                        return e.dma_start(out=qrow_s[0:1, t0:t0 + TT], in_=rows8.ap()[2:4, :])
                p.op("gpsimd", _row_q, reads=[rows8], dma=True)
            p.op("tensor", lambda e: e.matmul(ps_bc.ap(), ones_f1.ap(), row1.ap()[0:1, 0:TT],
                                              start=True, stop=True),
                 reads=[ones_f1, row1], writes=[ps_bc])
            p.op("tensor", lambda e: e.matmul(ps_Q.ap(), ones_f1.ap(), row1.ap()[0:1, TT:2 * TT],
                                              start=True, stop=True),
                 reads=[ones_f1, row1], writes=[ps_Q])
            return ps_bc, ps_Q

        mu_p = None

        def ph3ai_tile(i, t0):
            h2o = kt; yq = vt; rp = ek; rq = ekv
            p.op("sync", lambda e: e.dma_start(out=x0.ap(), in_=out1_s3[:, :, t0:t0 + TT]),
                 writes=[x0], dma=True)
            p.op("gpsimd", lambda e: e.tensor_copy(x0b.ap(), x0.ap()), reads=[x0], writes=[x0b])
            p.op("scalar", lambda e: e.activation(xsqb.ap(), x0.ap(), AF.Square), reads=[x0], writes=[xsqb])
            stats_n(x0b, xsqb, TT, ps_S, ps_Q)
            rows_plain(TT, r_rho, r_mrho)
            p.op("gpsimd", lambda e: e.tensor_mul(h2o.ap(), x0.ap(), bcast_g(r_rho)),
                 reads=[x0, r_rho], writes=[h2o])
            p.op("gpsimd", lambda e: e.tensor_sub(h2o.ap(), h2o.ap(), bcast_g(r_mrho)),
                 reads=[h2o, r_mrho], writes=[h2o])
            p.op("vector", lambda e: e.tensor_reduce(
                r_v0.ap().unsqueeze(2), h2o.ap().transpose([0, 2, 1]), mybir.AxisListType.X,
                OP.max, apply_absolute_value=True), reads=[h2o], writes=[r_v0])
            inv_b, deq_b = amax_machinery(
                r_v0,
                [lambda e: e.tensor_scalar_max(amax_tok.ap()[:, 0:2], amax_tok.ap()[:, 0:2], 1e-5)],
                [lambda e: e.tensor_scalar_mul(amax_tok.ap()[:, 4:6], amax_tok.ap()[:, 4:6], 127.0),
                 lambda e: e.tensor_scalar_mul(amax_tok.ap()[:, 2:4], amax_tok.ap()[:, 0:2], 1.0 / 127.0)])
            p.op("vector", lambda e, ib=inv_b: e.tensor_mul(
                yq.ap(), h2o.ap(), ib.ap().unsqueeze(1).broadcast_to([128, G, TT])),
                reads=[h2o, inv_b], writes=[yq])
            p.op("vector", lambda e: e._custom_dve(self.ops["ATH_RINT"], out=rq.ap(), in0=yq.ap(),
                                                   s0=12582912.0, s1=12582912.0), reads=[yq], writes=[rq])
            p.op("vector", lambda e, db=deq_b: e.tensor_mul(
                xvb.ap(), rq.ap(), db.ap().unsqueeze(1).broadcast_to([128, G, TT])),
                reads=[rq, deq_b], writes=[xvb])
            p.op("scalar", lambda e: e.dma_start(out=xhat1_s3[:, :, t0:t0 + TT], in_=xvb.ap()),
                 reads=[xvb], dma=True)
            for fg in range(FG):
                ps = rot5[fg % 5]
                def _gmm(e, ps=ps, fg=fg):
                    for cg in range(G):
                        inst = e.matmul(ps.ap(), ware.ap()[:, cg, fg * 128:(fg + 1) * 128],
                                        xvb.ap()[:, cg, :], start=(cg == 0), stop=(cg == G - 1))
                    return inst
                p.op("tensor", _gmm, reads=[ware, xvb], writes=[ps])
                ch = (fg // G) % 2
                dst = st_sl[ch][fg % G]
                p.op("scalar", lambda e, ps=ps, dst=dst: e.activation(
                    dst.ap(), ps.ap(), AF.Copy), reads=[ps], writes=[dst, st_ch[ch]])
                if fg % G == G - 1:
                    p.op("scalar", lambda e, fg=fg, ch=ch: e.dma_start(
                        out=gpre_s3[:, fg - G + 1:fg + 1, t0:t0 + TT], in_=st_ch[ch].ap()),
                        reads=st_sl[ch] + [st_ch[ch]], dma=True)

        for i in range(NT):
            ph3ai_tile(i, i * TT)
            self._wq1["d"].step(4)
        if stop_after == "ph3ai":
            self._finish(p, sems)
            return nc

        p.barrier()
        self._wq1["u"].flush()
        mu_p, mu_s = wquant(wuT, G, "u", 64)
        md_row = self.sbuf("md_row", (1, 128))

        gu_sl = [Buf(big32.ap()[:, q, :], f"gu_{q}") for q in range(FG)]
        upv_sl = [Buf(ktd[0].ap()[:, 4 + j, :], f"upv{j}") for j in range(4)]

        def ph3aii_tile(i, t0):
            xh1 = x0b
            p.op("sync", lambda e: e.dma_start(out=xh1.ap(), in_=xhat1_s3[:, :, t0:t0 + TT]),
                 writes=[xh1], dma=True)
            p.op("sync", lambda e: e.dma_start(out=big32.ap(), in_=gpre_s3[:, :, t0:t0 + TT]),
                 writes=[big32] + gu_sl, dma=True)
            for fg in range(FG):
                ps = rot5[fg % 5]
                def _umm(e, ps=ps, fg=fg):
                    for cg in range(G):
                        inst = e.matmul(ps.ap(), ware.ap()[:, cg, fg * 128:(fg + 1) * 128],
                                        xh1.ap()[:, cg, :], start=(cg == 0), stop=(cg == G - 1))
                    return inst
                p.op("tensor", _umm, reads=[ware, xh1], writes=[ps])
                sil, upv = rl_sl[fg % 4], upv_sl[fg % 4]
                p.op("scalar", lambda e, fg=fg, sil=sil: e.activation(
                    sil.ap(), big32.ap()[:, fg, :], AF.Silu, scale=mg_p.ap()),
                    reads=[big32, mg_p], writes=[sil])
                p.op("scalar", lambda e, ps=ps, upv=upv: e.activation(
                    upv.ap(), ps.ap(), AF.Copy), reads=[ps], writes=[upv])
                p.op("gpsimd", lambda e, fg=fg, sil=sil, upv=upv: e.tensor_mul(
                    big32.ap()[:, fg, :], sil.ap(), upv.ap()),
                    reads=[sil, upv], writes=[gu_sl[fg]])
            p.op("vector", lambda e: e.tensor_reduce(
                r_v0.ap().unsqueeze(2), big32.ap().transpose([0, 2, 1]), mybir.AxisListType.X,
                OP.max, apply_absolute_value=True), reads=gu_sl, writes=[r_v0])
            inv_b, deq_b = amax_machinery(
                r_v0,
                [lambda e: e.tensor_scalar_mul(amax_tok.ap()[:, 0:2], amax_tok.ap()[:, 0:2], mu_p.ap()),
                 lambda e: e.tensor_scalar_max(amax_tok.ap()[:, 0:2], amax_tok.ap()[:, 0:2], 1e-5)],
                [lambda e: e.tensor_scalar(amax_tok.ap()[:, 4:6], amax_tok.ap()[:, 4:6],
                                           mu_p.ap(), 127.0, OP.mult, OP.mult),
                 lambda e: e.tensor_scalar_mul(amax_tok.ap()[:, 2:4], amax_tok.ap()[:, 0:2], 1.0 / 127.0)],
                qrow_t0=t0)
            for ch in range(4):
                yq = vt
                stage = xsqb if ch % 2 == 0 else x0b
                p.op("vector", lambda e, ch=ch, ib=inv_b: e.tensor_mul(
                    yq.ap(), big32.ap()[:, 8 * ch:8 * ch + 8, :],
                    ib.ap().unsqueeze(1).broadcast_to([128, G, TT])),
                    reads=gu_sl[8 * ch:8 * ch + 8] + [inv_b], writes=[yq])
                p.op("vector", lambda e, stage=stage: e._custom_dve(
                    self.ops["ATH_RINT"], out=stage.ap(), in0=yq.ap(),
                    s0=12582912.0, s1=12582912.0), reads=[yq], writes=[stage])
                p.op("scalar", lambda e, ch=ch, stage=stage: e.dma_start(
                    out=xq2_s3[:, 8 * ch:8 * ch + 8, t0:t0 + TT], in_=stage.ap()),
                    reads=[stage], dma=True)

        for i in range(NT):
            ph3aii_tile(i, i * TT)
        if stop_after == "ph3aii":
            self._finish(p, sems)
            return nc

        p.barrier()
        self._wq1["d"].flush()
        md_p, md_s = wquant(wdT, FG, "d", 128)
        wd32 = Buf(ware.ap().rearrange("p g (x d) -> p (g x) d", x=4), "wd32", track=ware)
        out_3 = out.rearrange("(g p) t -> p g t", p=128)
        mdf = self.sbuf("mdf", (1, 1))
        def ph3b_tile(i, t0):
            ft = dd; o1t = ek; tmp = kt
            xqt = Buf(big32.ap()[:, 0:32, :], "xqt", track=big32)
            p.op("sync", lambda e, k=xqt: e.dma_start(
                out=k.ap(), in_=xq2_s3[:, :, t0:t0 + TT]), writes=[big32], dma=True)
            for cg in range(G):
                ps = ps_m0 if cg % 2 == 0 else ps_m1
                def _dmm(e, cg=cg, ps=ps, k=xqt):
                    for fg in range(FG):
                        inst = e.matmul(ps.ap(), wd32.ap()[:, fg, cg * 128:(cg + 1) * 128],
                                        k.ap()[:, fg, :], start=(fg == 0), stop=(fg == FG - 1))
                    return inst
                p.op("tensor", _dmm, reads=[wd32, big32], writes=[ps])
                p.op("vector", lambda e, cg=cg, ps=ps: e.tensor_copy(
                    ps_kv.ap()[:, cg, :], ps.ap()), reads=[ps, ps_kv], writes=[ps_kv])
            p.op("sync", lambda e: e.dma_start(out=row1.ap()[:, 0:TT], in_=qrow_s[0:1, t0:t0 + TT]),
                 writes=[row1], dma=True)
            p.op("vector", lambda e: e.tensor_scalar_mul(row1.ap()[:, 0:TT], row1.ap()[:, 0:TT],
                                                         md_s.ap()), reads=[row1, md_s], writes=[row1])
            p.op("tensor", lambda e: e.matmul(ps_bc.ap(), ones_f1.ap(), row1.ap()[0:1, 0:TT],
                                              start=True, stop=True),
                 reads=[ones_f1, row1], writes=[ps_bc])
            p.op("scalar", lambda e: e.activation(r_msq.ap(), ps_bc.ap(), AF.Copy),
                 reads=[ps_bc], writes=[r_msq])
            p.op("vector", lambda e: e.tensor_mul(
                ft.ap(), ps_kv.ap(), r_msq.ap().unsqueeze(1).broadcast_to([128, G, TT])),
                reads=[ps_kv, r_msq], writes=[ft])
            fb = Buf(big32.ap()[:, 0:G, :], "fb", track=big32)
            fsq = Buf(big32.ap()[:, 8:16, :], "fsq", track=big32)
            p.op("gpsimd", lambda e, d=fb: e.tensor_copy(d.ap(), ft.ap()), reads=[ft], writes=[big32])
            p.op("scalar", lambda e, d=fsq: e.activation(d.ap(), ft.ap(), AF.Square),
                 reads=[ft], writes=[big32])
            stats_n(fb, fsq, TT, ps_S, ps_Q)
            rows_plain(TT, r_rho, r_mrho)
            p.op("sync", lambda e: e.dma_start(out=o1t.ap(), in_=out1_s3[:, :, t0:t0 + TT]),
                 writes=[o1t], dma=True)
            p.op("gpsimd", lambda e: e.tensor_mul(tmp.ap(), ft.ap(), bcast_g(r_rho)),
                 reads=[ft, r_rho], writes=[tmp])
            p.op("gpsimd", lambda e: e.tensor_sub(tmp.ap(), tmp.ap(), bcast_g(r_mrho)),
                 reads=[tmp, r_mrho], writes=[tmp])
            p.op("gpsimd", lambda e: e.tensor_add(tmp.ap(), tmp.ap(), o1t.ap()),
                 reads=[tmp, o1t], writes=[tmp])
            p.op("scalar", lambda e: e.dma_start(out=out_3[:, :, t0:t0 + TT], in_=tmp.ap()),
                 reads=[tmp], dma=True)

        if "ph3dbg" in self.debug_dump:
            p.barrier()
            o5_ = self.dram_out("dbg_xhat1", (C, TOWN), BF16)
            p.op("sync", lambda e, o=o5_: e.dma_start(out=o, in_=xhat1_s[:, :]), dma=True)
            o6_ = self.dram_out("dbg_gpre", (F, TOWN), BF16)
            for _c in range(8):
                p.op("sync", lambda e, o=o6_, c=_c: e.dma_start(
                    out=o[c * 512:(c + 1) * 512, :], in_=gpre_s[c * 512:(c + 1) * 512, :]), dma=True)
            o7_ = self.dram_out("dbg_xq2", (F, TOWN), BF16)
            for _c in range(8):
                p.op("sync", lambda e, o=o7_, c=_c: e.dma_start(
                    out=o[c * 512:(c + 1) * 512, :], in_=xq2_s[c * 512:(c + 1) * 512, :]), dma=True)
            o8_ = self.dram_out("dbg_qrow", (1, TOWN))
            p.op("sync", lambda e, o=o8_: e.dma_start(out=o, in_=qrow_s[:, :]), dma=True)

        for i in range(NT):
            ph3b_tile(i, i * TT)

        self._finish(p, sems)
        return nc

    def ph2_ph3(self, p, sems, env):
        pass

    def _finish(self, p, sems):
        nc = self.nc
        snap = dict(p.cnt)
        p.q["sync"].append((None, {s: c for s, c in snap.items() if c > 0}, None, 0, "final"))
        import contextlib
        es = contextlib.ExitStack()
        for s in list(ALL_DMA_SEMS) + ["v", "a", "t", "p", "sq"]:
            sems[s] = es.enter_context(nc.semaphore(s))
        block = es.enter_context(nc.Block())
        with es:
            p.emit(block, sems)
        mybir.codegen_inst_isa_subclasses(nc)
        return nc

def kernel(**inputs):
    inp = {k: np.asarray(v) for k, v in inputs.items()}
    x = inp["L0_attn_x"]
    B, T, Cdim = x.shape
    assert (B, T, Cdim) == (4, 4096, 1024)

    cv = []
    for nm in ("tmk", "tmv", "tmr"):
        v = inp[f"L0_attn_{nm}"].astype(np.float32)
        cv.append(1.0 - v); cv.append(v)
    cv.append(inp["L0_attn_ln1_w"]); cv.append(inp["L0_attn_ln1_b"])
    cv.append(inp["L0_attn_ln2_w"]); cv.append(inp["L0_attn_ln2_b"])
    for nm in ("f_tmk", "f_tmr"):
        v = inp[f"L0_attn_{nm}"].astype(np.float32)
        cv.append(1.0 - v); cv.append(v)
    cv.append(inp["L0_attn_time_decay"]); cv.append(inp["L0_attn_time_first"])
    cvec = np.concatenate([np.asarray(v, np.float32).reshape(8, 128).T for v in cv],
                          axis=1)
    cvec = np.ascontiguousarray(cvec)

    shared = {
        "iden": np.eye(128, dtype=np.float32),
        "cvec": cvec,
        "wkT": np.ascontiguousarray(inp["L0_attn_att_kw"].T),
        "wvT": np.ascontiguousarray(inp["L0_attn_att_vw"].T),
        "wrT": np.ascontiguousarray(inp["L0_attn_att_rw"].T),
        "woT": np.ascontiguousarray(inp["L0_attn_att_ow"].T),
        "wfkT": np.ascontiguousarray(inp["L0_attn_f_kw"].T),
        "wfvT": np.ascontiguousarray(inp["L0_attn_f_vw"].T),
        "wfrT": np.ascontiguousarray(inp["L0_attn_f_rw"].T),
        "wgT": np.ascontiguousarray(inp["L0_ffn_w_gate"].T),
        "wuT": np.ascontiguousarray(inp["L0_ffn_w_up"].T),
        "wdT": np.ascontiguousarray(inp["L0_ffn_w_down"].T),
    }

    in_maps = []
    for core in range(8):
        b, half = core // 2, core % 2
        xt = np.ascontiguousarray(x[b].T)
        own = np.ascontiguousarray(xt[:, half * TOWN:(half + 1) * TOWN])
        pre = np.ascontiguousarray(xt[:, 0:TOWN]) if half == 1 else own
        m = dict(shared)
        m["x_pre"] = pre
        m["x_own"] = own
        m["gates"] = np.full((128, G), 0.0 if half == 1 else -1e30, np.float32)
        m["bgate"] = np.full((128, G), 1.0 if half == 1 else 0.0, np.float32)
        in_maps.append(m)

    mk = MK()
    nc = mk.build()
    res = run_bass_kernel_spmd(nc, in_maps, core_ids=list(range(8)))

    out = np.empty((B, T, Cdim), np.float32)
    for core in range(8):
        b, half = core // 2, core % 2
        shard = np.asarray(res.results[core]["out"])
        out[b, half * TOWN:(half + 1) * TOWN, :] = shard.T
    return out


# version 16
# speedup vs baseline: 1.0496x; 1.0496x over previous
from concourse.bass_utils import run_bass_kernel_spmd

"""Megakernel builder: mk_sandwich_d1 RWKV block, feature-major, 8-core SPMD.

Layout: channels on partitions in G=8 groups of 128; tokens on the free axis.
Each core owns (batch b = core//2, half = core%2) -> 2048 tokens; every core
also runs the k/v/scan prefix pipeline over the 2048 "prefix" tokens (gated to
a no-op on even cores via lngate = -1e30 so exp(k - m + lngate) = 0).

See /root/problem/NOTES.md for the full design.
"""
import numpy as np
import concourse.bass as bass
import concourse.mybir as mybir

F32 = mybir.dt.float32
BF16 = mybir.dt.bfloat16
AF = mybir.ActivationFunctionType
OP = mybir.AluOpType

G = 8
FG = 32
C = 1024
F = 4096
TOWN = 2048
EPS = 1e-5

_CUSTOM = {}


def _np_lvl(r, c):
    return r - (r >= c) * c


def make_custom_ops():
    global _CUSTOM
    if _CUSTOM:
        return _CUSTOM
    from concourse.dve_spec import Spec, Src0, Src1, C0, C1, C2, lower
    from concourse.dve_op import DveOp
    from concourse.dve_table_gen import dve_ver_for
    import concourse.dve_ops as dmod

    def lvl(r, c):
        return r - (r >= c) * c

    def mk(name, body, reference):
        ver = dve_ver_for("TRN2")
        sha = lower(Spec(body=body), ver=ver, perf_en=False, name=name).sha(ver)
        return DveOp(name, Spec(body=body), subdim=False, uops_sha={ver: sha},
                     reference=reference)

    ops = {}
    ops["ATH_TERN"] = mk("ATH_TERN", (Src0 >= C0) - (C1 >= Src0),
        lambda in0, in1, s0, s1, imm2: (in0 >= s0) * 1.0 - (in0 <= s1))
    ops["ATH_RINT"] = mk("ATH_RINT", (Src0 + C0) - C1,
        lambda in0, in1, s0, s1, imm2: np.rint(in0))

    names = [o.name for o in dmod.OPS]
    repl = [i for i, n in enumerate(names) if n.startswith("DVEHASH32L")]
    have = set(names)
    for op in ops.values():
        if op.name in have:
            continue
        if repl:
            dmod.OPS[repl.pop(0)] = op
        else:
            dmod.OPS.append(op)
    assert len(dmod.OPS) <= 31
    dmod._SUB_OPCODE_FOR_NAME.clear()
    dmod._SUB_OPCODE_FOR_NAME.update(
        {op.name: dmod._CUSTOM_DVE_ROW_BASE + i for i, op in enumerate(dmod.OPS)})
    dmod.CUSTOM_DVE_SPECS.clear()
    dmod.CUSTOM_DVE_SPECS.update(
        {op.name: op.spec for op in dmod.OPS if op.spec is not None})
    dmod.CUSTOM_DVE_REFS.clear()
    dmod.CUSTOM_DVE_REFS.update({op.name: op.ref for op in dmod.OPS})
    _CUSTOM = ops
    return ops


ENGINES = ("sync", "vector", "scalar", "tensor", "gpsimd")
DMA_LANES = 4
DMA_QUEUES = {"sync": "dS", "gpsimd": "dG", "scalar": "dA"}
ALL_DMA_SEMS = [q + str(i) for q in ("dS", "dG", "dA") for i in range(DMA_LANES)]


class Buf:
    __slots__ = ("ap_fn", "ready", "readers", "name", "track")

    def __init__(self, ap_fn, name="", track=None):
        self.ap_fn = ap_fn
        self.ready = {}
        self.readers = {}
        self.name = name
        self.track = track

    def ap(self):
        return self.ap_fn() if callable(self.ap_fn) else self.ap_fn


class Prog:
    serial = False
    serial_engines = set()

    def __init__(self, nc):
        self.nc = nc
        self.q = {e: [] for e in ENGINES}
        self.cnt = {s: 0 for s in ALL_DMA_SEMS}
        self.cnt.update({"v": 0, "a": 0, "t": 0, "p": 0, "sq": 0})
        self.esem = {"sync": "sq", "vector": "v", "scalar": "a",
                     "tensor": "t", "gpsimd": "p"}
        self.dma_issue = {"dS": 0, "dG": 0, "dA": 0}

    def op(self, eng, fn, reads=(), writes=(), dma=False, tag=""):
        waits = {}
        if dma:
            qn = DMA_QUEUES[eng]
            lane = self.dma_issue[qn] % DMA_LANES
            self.dma_issue[qn] += 1
            my_sem = qn + str(lane)
        else:
            my_sem = self.esem[eng]
        reads = [b.track or b for b in reads]
        writes = [b.track or b for b in writes]
        skip_own = (eng == "tensor") and not dma
        for b in reads:
            for s, c in b.ready.items():
                if skip_own and s == "t":
                    continue
                waits[s] = max(waits.get(s, 0), c)
        for b in writes:
            for src in (b.ready, b.readers):
                for s, c in src.items():
                    if skip_own and s == "t":
                        continue
                    waits[s] = max(waits.get(s, 0), c)
        if self.serial or eng in self.serial_engines:
            waits = {s: c for s, c in self.cnt.items() if c > 0}
        if dma and self.cnt[my_sem] > 0:
            waits[my_sem] = max(waits.get(my_sem, 0), self.cnt[my_sem])
        inc = 16 if dma else 1
        self.cnt[my_sem] += inc
        tok = (my_sem, self.cnt[my_sem])
        for b in writes:
            b.ready = {tok[0]: tok[1]}
            b.readers = {}
        for b in reads:
            b.readers[tok[0]] = tok[1]
        self.q[eng].append((fn, waits, my_sem, inc, tag))

    def barrier(self):
        snap = dict(self.cnt)
        for e in ENGINES:
            self.q[e].append((None, {s: c for s, c in snap.items() if c > 0},
                              None, 0, "barrier"))

    def emit(self, block, sems):
        emitted_waits = {e: {} for e in ENGINES}

        def run(eng_name, eng):
            my_csem = self.esem[eng_name]
            for fn, waits, my_sem, inc, tag in self.q[eng_name]:
                for s, c in sorted(waits.items()):
                    w = emitted_waits[eng_name]
                    if w.get(s, 0) >= c:
                        continue
                    eng.wait_ge(sems[s], c)
                    w[s] = c
                if fn is None:
                    continue
                inst = fn(eng)
                if inc:
                    assert inst is not None, f"op returned no instruction: {tag}"
                    inst.then_inc(sems[my_sem], inc)

        for ename in ENGINES:
            dec = getattr(block, ename)
            dec(lambda eng, _n=ename: run(_n, eng))



def ceil_div(a, b):
    return (a + b - 1) // b


class MK:
    def __init__(self, debug_dump=()):
        self.ops = make_custom_ops()
        self.debug_dump = set(debug_dump)
        self.nc = bass.Bass("TRN2", target_bir_lowering=False, debug=False)
        self.inputs = {}
        self.outputs = {}
        self.sb = {}
        self._sb_tensors = []
        self._consts = {}
        self._dbg = {}

    def dram_in(self, name, shape, dtype=F32):
        t = self.nc.dram_tensor(name, list(shape), dtype, kind="ExternalInput")
        self.inputs[name] = t
        return t.ap()

    def dram_out(self, name, shape, dtype=F32):
        t = self.nc.dram_tensor(name, list(shape), dtype, kind="ExternalOutput")
        self.outputs[name] = t
        return t.ap()

    def dram_tmp(self, name, shape, dtype=F32):
        return self.nc.dram_tensor(name, list(shape), dtype).ap()

    def sbuf(self, name, shape, dtype=F32):
        t = self.nc.alloc_sbuf_tensor(name, list(shape), dtype)
        self._sb_tensors.append(t)
        return Buf(t.ap(), name)

    def psum(self, name, shape, dtype=F32):
        t = self.nc.alloc_psum_tensor(name, list(shape), dtype)
        return Buf(t.ap(), name)

    def dbg(self, name, buf_ap, shape, dtype=F32):
        o = self.dram_out("dbg_" + name, shape, dtype)
        return o

    def constf(self, value):
        key = float(value)
        if key in self._consts:
            return self._consts[key]
        b = self.sbuf(f"cst_{len(self._consts)}", (128, 1))
        self.p.op("vector", lambda e: e.memset(b.ap(), key), writes=[b])
        self.nc.const_aps.register(F32, key, b.ap())
        self._consts[key] = b
        return b

    def build(self, stop_after="all"):
        nc = self.nc
        p = Prog(nc)
        self.p = p
        TT = 256
        NT = TOWN // TT
        NP = NT
        self.p = p

        x_pre = self.dram_in("x_pre", (C, TOWN))
        x_own = self.dram_in("x_own", (C, TOWN))
        iden = self.dram_in("iden", (128, 128))
        cvec = self.dram_in("cvec", (128, 8 * 16))
        gates = self.dram_in("gates", (128, G))
        bgate = self.dram_in("bgate", (128, G))
        wkT = self.dram_in("wkT", (C, C))
        wvT = self.dram_in("wvT", (C, C))
        wrT = self.dram_in("wrT", (C, C))
        woT = self.dram_in("woT", (C, C))
        wfkT = self.dram_in("wfkT", (C, F))
        wfvT = self.dram_in("wfvT", (F, C))
        wfrT = self.dram_in("wfrT", (C, C))
        wgT = self.dram_in("wgT", (C, F))
        wuT = self.dram_in("wuT", (C, F))
        wdT = self.dram_in("wdT", (F, C))

        out = self.dram_out("out", (C, TOWN))

        x1_s = self.dram_tmp("x1_s", (C, TOWN + 1))
        xk2_s = self.dram_tmp("xk2_s", (C, TOWN), BF16)
        sr2_s = self.dram_tmp("sr2_s", (C, TOWN), BF16)
        out1_s = self.dram_tmp("out1_s", (C, TOWN))
        xhat1_s = self.dram_tmp("xhat1_s", (C, TOWN), BF16)
        gpre_s = self.dram_tmp("gpre_s", (F, TOWN), BF16)
        kk_s = self.dram_tmp("kk_s", (F, TOWN), BF16)
        xq2_s = self.dram_tmp("xq2_s", (F, TOWN), BF16)
        qrow_s = self.dram_tmp("qrow_s", (1, TOWN))
        x1_s3 = x1_s.rearrange("(g p) t -> p g t", p=128)

        ones_bf = self.sbuf("ones_bf", (128, 128), BF16)
        ident = self.sbuf("ident", (128, 128))
        cv = self.sbuf("cv", (128, 8 * 16))
        gate_t = self.sbuf("gate_t", (128, G))
        bg_t = self.sbuf("bg_t", (128, G))
        lam = self.sbuf("lam", (128, G))
        zero8 = self.sbuf("zero8", (128, G))

        def cvs(i):
            return cv.ap()[:, i * G:(i + 1) * G]
        CV = {n: i for i, n in enumerate(
            ["tmk1", "tmk", "tmv1", "tmv", "tmr1", "tmr", "g1", "b1",
             "g2", "b2", "ftmk1", "ftmk", "ftmr1", "ftmr", "td", "u"])}

        sems = {}
        self._sems = sems

        p.op("sync", lambda e: e.dma_start(out=ident.ap(), in_=iden), writes=[ident], dma=True)
        p.op("sync", lambda e: e.dma_start(out=cv.ap(), in_=cvec), writes=[cv], dma=True)
        p.op("sync", lambda e: e.dma_start(out=gate_t.ap(), in_=gates), writes=[gate_t], dma=True)
        p.op("sync", lambda e: e.dma_start(out=bg_t.ap(), in_=bgate), writes=[bg_t], dma=True)
        p.op("vector", lambda e: e.memset(ones_bf.ap(), 1.0), writes=[ones_bf])
        p.op("vector", lambda e: e.memset(zero8.ap(), 0.0), writes=[zero8])
        c_eps = self.constf(EPS)
        c_one = self.constf(1.0)
        t8 = self.sbuf("t8", (128, G))
        p.op("scalar", lambda e: e.activation(t8.ap(), cvs(CV["td"]), AF.Exp),
             reads=[cv], writes=[t8])
        p.op("scalar", lambda e: e.activation(lam.ap(), t8.ap(), AF.Exp, scale=-1.0),
             reads=[t8], writes=[lam])

        ware = self.sbuf("ware", (128, G, 4096), BF16)
        def wview(ap, nm):
            return Buf(ap, nm, track=ware)
        wk_sb = wview(ware.ap()[:, :, 0:C], "wk")
        wv_sb = wview(ware.ap()[:, :, C:2 * C], "wv")
        wr_sb = wview(ware.ap()[:, :, 2 * C:3 * C], "wr")
        wo_sb = wview(ware.ap()[:, :, 3 * C:4 * C], "wo")
        for w_sb, w_h in ((wk_sb, wkT), (wv_sb, wvT), (wr_sb, wrT), (wo_sb, woT)):
            p.op("gpsimd", lambda e, a=w_sb, b=w_h: e.dma_start(
                out=a.ap(), in_=b.rearrange("(g p) d -> p g d", p=128)),
                writes=[w_sb], dma=True)

        x0 = self.sbuf("x0", (128, G, TT))
        x0b = self.sbuf("x0b", (128, G, TT), BF16)
        xsqb = self.sbuf("xsqb", (128, G, TT), BF16)
        h = [self.sbuf("h0", (128, G, TT + 1), BF16),
             self.sbuf("h1", (128, G, TT + 1), BF16)]
        hind = [self.sbuf("hinA", (128, G, TT)), self.sbuf("hinB", (128, G, TT))]
        dd = self.sbuf("dd", (128, G, TT))
        big32 = self.sbuf("big32", (128, 32, TT), BF16)
        rwkb = Buf(big32.ap()[:, 0:G, :], "rwkb")
        xkb = Buf(big32.ap()[:, G:2 * G, :], "xkb")
        xvb = Buf(big32.ap()[:, 2 * G:3 * G, :], "xvb")
        xrb = Buf(big32.ap()[:, 3 * G:4 * G, :], "xrb")
        ktd = [self.sbuf("ktA", (128, G, TT)), self.sbuf("ktB", (128, G, TT))]
        vtd = [self.sbuf("vtA", (128, G, TT)), self.sbuf("vtB", (128, G, TT))]
        srd = [self.sbuf("srA", (128, G, TT)), self.sbuf("srB", (128, G, TT))]
        ek = self.sbuf("ek", (128, G, TT))
        ekv = self.sbuf("ekv", (128, G, TT))
        hin, kt, vt, sr = hind[0], ktd[0], vtd[0], srd[0]
        a_st = self.sbuf("a_st", (128, G))
        b_st = self.sbuf("b_st", (128, G))
        m_st = self.sbuf("m_st", (128, G))
        rmax = self.sbuf("rmax", (128, G))
        mnew = self.sbuf("mnew", (128, G))
        ebias = self.sbuf("ebias", (128, G))
        ubias = self.sbuf("ubias", (128, G))
        sdel = self.sbuf("sdel", (128, G))
        srsc = self.sbuf("srsc", (128, G))
        r_msq = self.sbuf("r_msq", (128, TT))
        r_v0 = self.sbuf("r_v0", (128, TT))
        r_L = self.sbuf("r_L", (128, TT))
        r_r0 = self.sbuf("r_r0", (128, TT))
        r_inv = r_msq
        r_v1 = r_L
        r_rho = self.sbuf("r_rho", (128, TT))
        r_mrho = self.sbuf("r_mrho", (128, TT))
        r_mr0 = r_v0

        ps_S = self.psum("ps_S", (128, TT))
        ps_Q = self.psum("ps_Q", (128, TT))
        ps_m0 = self.psum("ps_m0", (128, TT))
        ps_m1 = self.psum("ps_m1", (128, TT))
        ps_kv = self.psum("ps_kv", (128, G, TT))
        p4 = [Buf(ps_kv.ap()[:, 2 * j, :], f"p4_{j}") for j in range(4)]

        p.op("vector", lambda e: e.memset(a_st.ap(), 0.0), writes=[a_st])
        p.op("vector", lambda e: e.memset(b_st.ap(), 0.0), writes=[b_st])
        p.op("vector", lambda e: e.memset(m_st.ap(), -1e30), writes=[m_st])

        def bcast_g(rtile, shape3=(128, G, TT)):
            return rtile.ap().unsqueeze(1).broadcast_to(list(shape3))

        dbg_outs = {}
        dbg_dt = {"rwkv": BF16}
        for nm in ("h", "hpre", "hin", "k", "v", "sr", "wkv", "rwkv", "idn", "npre", "ek2"):
            if nm in self.debug_dump:
                dbg_outs[nm] = self.dram_out("dbg_" + nm, (C, TOWN),
                                             dbg_dt.get(nm, F32)).rearrange(
                    "(g p) t -> p g t", p=128)

        dbg_stage = ekv

        def dump(nm, buf, ap3, t0, dense=True):
            if nm not in dbg_outs:
                return
            if not dense:
                p.op("vector", lambda e, a=ap3: e.tensor_copy(dbg_stage.ap(), a),
                     reads=[buf], writes=[dbg_stage])
                buf, ap3 = dbg_stage, dbg_stage.ap()
            p.op("scalar", lambda e, a=ap3, t=t0: e.dma_start(
                out=dbg_outs[nm][:, :, t:t + TT], in_=a), reads=[buf], dma=True)

        x1_s3 = x1_s.rearrange("(g p) t -> p g t", p=128)

        def ph1_s1(i, pre, full, src, t0, hc, hp):
            p.op("sync", lambda e, s=src, t=t0: e.dma_start(
                out=x0.ap(), in_=s.rearrange("(g p) t -> p g t", p=128)[:, :, t:t + TT]),
                writes=[x0], dma=True)
            p.op("scalar", lambda e: e.activation(x0b.ap(), x0.ap(), AF.Copy),
                 reads=[x0], writes=[x0b])
            p.op("scalar", lambda e: e.activation(xsqb.ap(), x0.ap(), AF.Square),
                 reads=[x0], writes=[xsqb])
            def _stats(e):
                for q, ps in ((x0b, ps_S), (xsqb, ps_Q)):
                    for g in range(G):
                        inst = e.matmul(ps.ap(), ones_bf.ap(), q.ap()[:, g, :],
                                        start=(g == 0), stop=(g == G - 1))
                return inst
            p.op("tensor", _stats, reads=[x0b, xsqb, ones_bf], writes=[ps_S, ps_Q])
            p.op("scalar", lambda e: e.activation(r_msq.ap(), ps_S.ap(), AF.Square, scale=1.0 / C),
                 reads=[ps_S], writes=[r_msq])
            p.op("vector", lambda e: e.scalar_tensor_tensor(
                r_v0.ap(), ps_Q.ap(), 1.0 / C, r_msq.ap(), OP.mult, OP.subtract),
                reads=[ps_Q, r_msq], writes=[r_v0])
            p.op("scalar", lambda e: e.activation(r_L.ap(), r_v0.ap(), AF.Ln, bias=EPS),
                 reads=[r_v0, c_eps], writes=[r_L])
            p.op("scalar", lambda e: e.activation(r_r0.ap(), r_L.ap(), AF.Exp, scale=-0.5),
                 reads=[r_L], writes=[r_r0])
            p.op("scalar", lambda e: e.activation(r_inv.ap(), r_L.ap(), AF.Exp, scale=-1.0),
                 reads=[r_L], writes=[r_inv])
            p.op("vector", lambda e: e.tensor_mul(r_v1.ap(), r_v0.ap(), r_inv.ap()),
                 reads=[r_v0, r_inv], writes=[r_v1])
            p.op("scalar", lambda e: e.activation(r_L.ap(), r_v1.ap(), AF.Ln, bias=EPS),
                 reads=[r_v1, c_eps], writes=[r_L])
            p.op("scalar", lambda e: e.activation(r_v1.ap(), r_L.ap(), AF.Exp, scale=-0.5),
                 reads=[r_L], writes=[r_v1])
            p.op("vector", lambda e: e.tensor_mul(r_rho.ap(), r_r0.ap(), r_v1.ap()),
                 reads=[r_r0, r_v1], writes=[r_rho])
            p.op("vector", lambda e: e.scalar_tensor_tensor(
                r_mrho.ap(), ps_S.ap(), 1.0 / C, r_rho.ap(), OP.mult, OP.mult),
                reads=[ps_S, r_rho], writes=[r_mrho])
            if full:
                p.op("vector", lambda e: e.scalar_tensor_tensor(
                    r_mr0.ap(), ps_S.ap(), 1.0 / C, r_r0.ap(), OP.mult, OP.mult),
                    reads=[ps_S, r_r0], writes=[r_mr0])

        def ph1_s2(i, pre, full, src, t0, hc, hp):
            kt, vt, sr, hin = ktd[i % 2], vtd[i % 2], srd[i % 2], hind[i % 2]
            p.op("vector", lambda e: e.tensor_mul(dd.ap(), x0.ap(), bcast_g(r_rho)),
                 reads=[x0, r_rho], writes=[dd])
            p.op("vector", lambda e: e.tensor_sub(dd.ap(), dd.ap(), bcast_g(r_mrho)),
                 reads=[dd, r_mrho], writes=[dd])
            def _gb(e):
                for g in range(G):
                    inst = e.tensor_scalar(
                        hc.ap()[:, g, 1:TT + 1], dd.ap()[:, g, :],
                        cvs(CV["g1"])[:, g:g + 1], cvs(CV["b1"])[:, g:g + 1],
                        OP.mult, OP.add)
                return inst
            p.op("vector", _gb, reads=[dd, cv], writes=[hc])
            if i == 0:
                p.op("vector", lambda e: e.memset(hc.ap()[:, :, 0:1], 0.0),
                     reads=[hc], writes=[hc])
            elif i == NP:
                p.op("vector", lambda e: e.tensor_mul(
                    hc.ap()[:, :, 0:1], hp.ap()[:, :, TT:TT + 1],
                    bg_t.ap().unsqueeze(2)),
                    reads=[hp, bg_t, hc], writes=[hc])
            else:
                p.op("vector", lambda e: e.tensor_copy(
                    hc.ap()[:, :, 0:1], hp.ap()[:, :, TT:TT + 1]),
                    reads=[hp, hc], writes=[hc])
            if not pre:
                dump("h", hc, hc.ap()[:, :, 1:TT + 1], t0, dense=False)
            p.op("gpsimd", lambda e: e.tensor_sub(
                dd.ap(), hc.ap()[:, :, 1:TT + 1], hc.ap()[:, :, 0:TT]),
                reads=[hc], writes=[dd])
            mixes = [(xkb, "tmk"), (xvb, "tmv")] + ([(xrb, "tmr")] if full else [])
            for dst, nm in mixes:
                def _mix(e, dst=dst, nm=nm):
                    for g in range(G):
                        inst = e.scalar_tensor_tensor(
                            dst.ap()[:, g, :], dd.ap()[:, g, :], cvs(CV[nm])[:, g:g + 1],
                            hc.ap()[:, g, 0:TT], OP.mult, OP.add)
                    return inst
                p.op("vector", _mix, reads=[dd, hc, cv], writes=[dst])
            grp = [0]
            def mm_pair(w_sb, src_b, dg, dst, ev_fn):
                ps = p4[grp[0] % 4]
                grp[0] += 1
                def _mm(e, ps=ps, w=w_sb, s=src_b, dg=dg):
                    for cg in range(G):
                        inst = e.matmul(ps.ap(), w.ap()[:, cg, dg * 128:(dg + 1) * 128],
                                        s.ap()[:, cg, :], start=(cg == 0), stop=(cg == G - 1))
                    return inst
                p.op("tensor", _mm, reads=[w_sb, src_b], writes=[ps])
                ev_fn(dst, ps, dg)

            def ev_dve(dst, ps, dg):
                p.op("scalar", lambda e, d=dst, ps=ps, dg=dg: e.activation(
                    d.ap()[:, dg, :], ps.ap(), AF.Copy), reads=[ps], writes=[dst])

            def ev_exp(dst, ps, dg):
                p.op("scalar", lambda e, d=dst, ps=ps, dg=dg: e.activation(
                    d.ap()[:, dg, :], ps.ap(), AF.Exp, scale=-1.0),
                    reads=[ps], writes=[dst])

            for dg in range(G):
                mm_pair(wk_sb, xkb, dg, kt, ev_dve)
            if full:
                p.op("gpsimd", lambda e, hin=hin: e.tensor_mul(hin.ap(), x0.ap(), bcast_g(r_r0)),
                     reads=[x0, r_r0], writes=[hin])
                p.op("gpsimd", lambda e, hin=hin: e.tensor_sub(hin.ap(), hin.ap(), bcast_g(r_mr0)),
                     reads=[hin, r_mr0], writes=[hin])
            for dg in range(G):
                mm_pair(wv_sb, xvb, dg, vt, ev_dve)
            if full:
                for dg in range(G):
                    mm_pair(wr_sb, xrb, dg, sr, ev_exp)
                p.op("scalar", lambda e, sr=sr: e.activation(sr.ap(), sr.ap(), AF.Ln, bias=1.0),
                     reads=[sr, c_one], writes=[sr])
                p.op("scalar", lambda e, sr=sr: e.activation(sr.ap(), sr.ap(), AF.Exp, scale=-1.0),
                     reads=[sr], writes=[sr])

        def ph1_back(i, pre, full, t0, s2_next=None, s1_next2=None):
            kt, vt, sr, hin = ktd[i % 2], vtd[i % 2], srd[i % 2], hind[i % 2]
            x1t = kt
            p.op("vector", lambda e, kt=kt: e.tensor_reduce(
                rmax.ap().unsqueeze(2), kt.ap(), mybir.AxisListType.X, OP.max),
                reads=[kt], writes=[rmax])
            p.op("vector", lambda e: e.tensor_max(mnew.ap(), m_st.ap(), rmax.ap()),
                 reads=[m_st, rmax], writes=[mnew])
            p.op("vector", lambda e: e.tensor_sub(sdel.ap(), m_st.ap(), mnew.ap()),
                 reads=[m_st, mnew], writes=[sdel])
            p.op("scalar", lambda e: e.activation(srsc.ap(), sdel.ap(), AF.Exp),
                 reads=[sdel], writes=[srsc])
            p.op("vector", lambda e: e.tensor_mul(a_st.ap(), a_st.ap(), srsc.ap()),
                 reads=[a_st, srsc], writes=[a_st])
            p.op("vector", lambda e: e.tensor_mul(b_st.ap(), b_st.ap(), srsc.ap()),
                 reads=[b_st, srsc], writes=[b_st])
            p.op("vector", lambda e: e.tensor_copy(m_st.ap(), mnew.ap()),
                 reads=[mnew], writes=[m_st])
            gb = gate_t if pre else zero8
            p.op("vector", lambda e, gb=gb: e.tensor_sub(ebias.ap(), gb.ap(), mnew.ap()),
                 reads=[gb, mnew], writes=[ebias])
            if full:
                p.op("vector", lambda e: e.tensor_sub(ubias.ap(), cvs(CV["u"]), mnew.ap()),
                     reads=[cv, mnew], writes=[ubias])

            def _ekx(e, kt=kt):
                for g in range(G):
                    inst = e.activation(ek.ap()[:, g, :], kt.ap()[:, g, :], AF.Exp,
                                        bias=ebias.ap()[:, g:g + 1])
                return inst
            p.op("scalar", _ekx, reads=[kt, ebias], writes=[ek])
            if full:
                def _ukx(e, kt=kt):
                    for g in range(G):
                        inst = e.activation(kt.ap()[:, g, :], kt.ap()[:, g, :], AF.Exp,
                                            bias=ubias.ap()[:, g:g + 1])
                    return inst
                p.op("scalar", _ukx, reads=[kt, ubias], writes=[kt])
            if s2_next is not None:
                s2_next()
            p.op("vector", lambda e, vt=vt: e.tensor_mul(ekv.ap(), ek.ap(), vt.ap()),
                 reads=[ek, vt], writes=[ekv])
            def _scanA(e):
                for g in range(G):
                    inst = e.tensor_tensor_scan(
                        ekv.ap()[:, g, :],
                        lam.ap()[:, g:g + 1].broadcast_to([128, TT]),
                        ekv.ap()[:, g, :], a_st.ap()[:, g:g + 1], OP.mult, OP.add)
                return inst
            p.op("vector", _scanA, reads=[ekv, lam, a_st], writes=[ekv])
            def _scanB(e):
                for g in range(G):
                    inst = e.tensor_tensor_scan(
                        ek.ap()[:, g, :],
                        lam.ap()[:, g:g + 1].broadcast_to([128, TT]),
                        ek.ap()[:, g, :], b_st.ap()[:, g:g + 1], OP.mult, OP.add)
                return inst
            p.op("vector", _scanB, reads=[ek, lam, b_st], writes=[ek])
            p.op("vector", lambda e: e.drain(), reads=[ekv, ek], writes=[])
            if not full:
                p.op("vector", lambda e: e.tensor_copy(a_st.ap().unsqueeze(2), ekv.ap()[:, :, TT - 1:TT]),
                     reads=[ekv], writes=[a_st])
                p.op("vector", lambda e: e.tensor_copy(b_st.ap().unsqueeze(2), ek.ap()[:, :, TT - 1:TT]),
                     reads=[ek], writes=[b_st])
                if s1_next2 is not None:
                    s1_next2()
                return
            p.op("gpsimd", lambda e, vt=vt, kt=kt: e.tensor_mul(vt.ap(), vt.ap(), kt.ap()),
                 reads=[vt, kt], writes=[vt])
            p.op("gpsimd", lambda e, vt=vt: e.tensor_add(
                vt.ap()[:, :, 1:TT], vt.ap()[:, :, 1:TT], ekv.ap()[:, :, 0:TT - 1]),
                reads=[vt, ekv], writes=[vt])
            p.op("gpsimd", lambda e, vt=vt: e.tensor_add(
                vt.ap()[:, :, 0:1], vt.ap()[:, :, 0:1], a_st.ap().unsqueeze(2)),
                reads=[vt, a_st], writes=[vt])
            p.op("vector", lambda e, kt=kt: e.tensor_add(
                ekv.ap()[:, :, 0:1], kt.ap()[:, :, 0:1], b_st.ap().unsqueeze(2)),
                reads=[kt, b_st, ekv], writes=[ekv])
            p.op("vector", lambda e: e.tensor_copy(a_st.ap().unsqueeze(2), ekv.ap()[:, :, TT - 1:TT]),
                 reads=[ekv], writes=[a_st])
            p.op("vector", lambda e: e.tensor_copy(b_st.ap().unsqueeze(2), ek.ap()[:, :, TT - 1:TT]),
                 reads=[ek], writes=[b_st])
            p.op("vector", lambda e, kt=kt: e.tensor_add(
                ekv.ap()[:, :, 1:TT], kt.ap()[:, :, 1:TT], ek.ap()[:, :, 0:TT - 1]),
                reads=[kt, ek, ekv], writes=[ekv])
            if not pre:
                dump("npre", vt, vt.ap(), t0)
                dump("ek2", kt, kt.ap(), t0)
            p.op("scalar", lambda e: e.activation(ekv.ap(), ekv.ap(), AF.Ln),
                 reads=[ekv], writes=[ekv])
            p.op("scalar", lambda e: e.activation(ekv.ap(), ekv.ap(), AF.Exp, scale=-1.0),
                 reads=[ekv], writes=[ekv])
            if not pre:
                dump("idn", ekv, ekv.ap(), t0)
            p.op("gpsimd", lambda e, vt=vt: e.tensor_mul(vt.ap(), vt.ap(), ekv.ap()),
                 reads=[vt, ekv], writes=[vt])
            if not pre:
                dump("wkv", vt, vt.ap(), t0)
            p.op("gpsimd", lambda e, vt=vt, sr=sr: e.tensor_mul(rwkb.ap(), vt.ap(), sr.ap()),
                 reads=[vt, sr], writes=[rwkb])
            if not pre:
                dump("rwkv", rwkb, rwkb.ap(), t0)
            for cg in range(G):
                ps = ps_m0 if cg % 2 == 0 else ps_m1
                def _omm(e, ps=ps, cg=cg):
                    for dg in range(G):
                        inst = e.matmul(ps.ap(), wo_sb.ap()[:, dg, cg * 128:(cg + 1) * 128],
                                        rwkb.ap()[:, dg, :], start=(dg == 0), stop=(dg == G - 1))
                    return inst
                p.op("tensor", _omm, reads=[wo_sb, rwkb], writes=[ps])
                p.op("vector", lambda e, ps=ps, cg=cg, x1t=x1t, hin=hin: e.tensor_add(
                    x1t.ap()[:, cg, :], ps.ap(), hin.ap()[:, cg, :]),
                    reads=[ps, hin, x1t], writes=[x1t])
            if pre:
                def _bd(e, h3=x1_s3, x1t=x1t):
                    with self.nc.allow_non_contiguous_dma(reason="4KB boundary col"):
                        return e.dma_start(out=h3[:, :, 0:1], in_=x1t.ap()[:, :, TT - 1:TT])
                p.op("gpsimd", _bd, reads=[x1t], dma=True)
            else:
                p.op("gpsimd", lambda e, t=t0, h3=x1_s3, x1t=x1t: e.dma_start(
                    out=h3[:, :, 1 + t:1 + t + TT], in_=x1t.ap()),
                    reads=[x1t], dma=True)
            if s1_next2 is not None:
                s1_next2()


        def ph1_args(j):
            return (j, j < NP, (j >= NP) or (j == NP - 1),
                    x_pre if j < NP else x_own,
                    (j if j < NP else j - NP) * TT,
                    h[j % 2], h[(j + 1) % 2])

        NTOT = NP + NT
        ph1_s1(*ph1_args(0))
        ph1_s2(*ph1_args(0))
        if NTOT > 1:
            ph1_s1(*ph1_args(1))
        for i in range(NTOT):
            s2n = (lambda j=i + 1: ph1_s2(*ph1_args(j))) if i + 1 < NTOT else None
            s1n = (lambda j=i + 2: ph1_s1(*ph1_args(j))) if i + 2 < NTOT else None
            a = ph1_args(i)
            ph1_back(a[0], a[1], a[2], a[4], s2_next=s2n, s1_next2=s1n)

        if "x1" in self.debug_dump:
            dbg = self.dram_out("dbg_x1", (C, TOWN + 1))
            p.barrier()
            p.op("sync", lambda e: e.dma_start(out=dbg, in_=x1_s[:, :]), dma=True)
        if stop_after == "ph1":
            self._finish(p, sems)
            return nc

        p.barrier()
        x1v = x1_s3
        xk2_s3 = xk2_s.rearrange("(g p) t -> p g t", p=128)
        sr2_s3 = sr2_s.rearrange("(g p) t -> p g t", p=128)
        out1_s3 = out1_s.rearrange("(g p) t -> p g t", p=128)
        xhat1_s3 = xhat1_s.rearrange("(g p) t -> p g t", p=128)
        gpre_s3 = gpre_s.rearrange("(q p) t -> p q t", p=128)
        kk_s3 = kk_s.rearrange("(q p) t -> p q t", p=128)
        xq2_s3 = xq2_s.rearrange("(q p) t -> p q t", p=128)
        ones_f1 = self.sbuf("ones_f1", (1, 128))
        p.op("vector", lambda e: e.memset(ones_f1.ap(), 1.0), writes=[ones_f1])
        h2bd = self.sbuf("h2bd", (128, G))
        wfr_sb = Buf(ware.ap()[:, :, 0:C], "wfr", track=ware)
        p.op("gpsimd", lambda e: e.dma_start(
            out=wfr_sb.ap(), in_=wfrT.rearrange("(g p) d -> p g d", p=128)),
            writes=[ware], dma=True)

        def stats_n(src_b, sq_b, n, ps1, ps2):
            def f(e, a=src_b, b=sq_b, n=n, ps1=ps1, ps2=ps2):
                for q, ps in ((a, ps1), (b, ps2)):
                    for g in range(G):
                        inst = e.matmul(ps.ap()[:, 0:n], ones_bf.ap(), q.ap()[:, g, 0:n],
                                        start=(g == 0), stop=(g == G - 1))
                return inst
            p.op("tensor", f, reads=[src_b, sq_b, ones_bf], writes=[ps1, ps2])

        def rows_plain(n, r_out, mr_out):
            p.op("scalar", lambda e: e.activation(r_msq.ap()[:, 0:n], ps_S.ap()[:, 0:n], AF.Square, scale=1.0 / C),
                 reads=[ps_S], writes=[r_msq])
            p.op("vector", lambda e: e.scalar_tensor_tensor(
                r_v0.ap()[:, 0:n], ps_Q.ap()[:, 0:n], 1.0 / C, r_msq.ap()[:, 0:n], OP.mult, OP.subtract),
                reads=[ps_Q, r_msq], writes=[r_v0])
            p.op("scalar", lambda e: e.activation(r_L.ap()[:, 0:n], r_v0.ap()[:, 0:n], AF.Ln, bias=EPS),
                 reads=[r_v0, c_eps], writes=[r_L])
            p.op("scalar", lambda e: e.activation(r_out.ap()[:, 0:n], r_L.ap()[:, 0:n], AF.Exp, scale=-0.5),
                 reads=[r_L], writes=[r_out])
            p.op("vector", lambda e: e.scalar_tensor_tensor(
                mr_out.ap()[:, 0:n], ps_S.ap()[:, 0:n], 1.0 / C, r_out.ap()[:, 0:n], OP.mult, OP.mult),
                reads=[ps_S, r_out], writes=[mr_out])

        x1bd = ekv
        x1bdb = x0b
        def _ldbd(e):
            with self.nc.allow_non_contiguous_dma(reason="4KB boundary col"):
                return e.dma_start(out=x1bd.ap()[:, :, 0:1], in_=x1v[:, :, 0:1])
        p.op("sync", _ldbd, writes=[x1bd], dma=True)
        p.op("gpsimd", lambda e: e.tensor_copy(x1bdb.ap()[:, :, 0:1], x1bd.ap()[:, :, 0:1]),
             reads=[x1bd], writes=[x1bdb])
        p.op("scalar", lambda e: e.activation(xsqb.ap()[:, :, 0:1], x1bd.ap()[:, :, 0:1], AF.Square),
             reads=[x1bd], writes=[xsqb])
        stats_n(x1bdb, xsqb, 1, ps_S, ps_Q)
        rows_plain(1, r_rho, r_mrho)
        def _bd1(e):
            for g in range(G):
                inst = e.tensor_scalar(
                    dd.ap()[:, g, 0:1], x1bd.ap()[:, g, 0:1], r_rho.ap()[:, 0:1],
                    r_mrho.ap()[:, 0:1], OP.mult, OP.subtract)
            return inst
        p.op("vector", _bd1, reads=[x1bd, r_rho, r_mrho], writes=[dd])
        def _bd2(e):
            for g in range(G):
                inst = e.tensor_scalar(
                    h2bd.ap()[:, g:g + 1], dd.ap()[:, g, 0:1],
                    cvs(CV["g2"])[:, g:g + 1], cvs(CV["b2"])[:, g:g + 1], OP.mult, OP.add)
            return inst
        p.op("vector", _bd2, reads=[dd, cv], writes=[h2bd])
        p.op("vector", lambda e: e.tensor_mul(h2bd.ap(), h2bd.ap(), bg_t.ap()),
             reads=[h2bd, bg_t], writes=[h2bd])

        def ph2a_tile(i, t0, hc, hp):
            p.op("sync", lambda e: e.dma_start(
                out=x0.ap(), in_=x1v[:, :, 1 + t0:1 + t0 + TT]), writes=[x0], dma=True)
            p.op("gpsimd", lambda e: e.tensor_copy(x0b.ap(), x0.ap()), reads=[x0], writes=[x0b])
            p.op("scalar", lambda e: e.activation(xsqb.ap(), x0.ap(), AF.Square), reads=[x0], writes=[xsqb])
            stats_n(x0b, xsqb, TT, ps_S, ps_Q)
            rows_plain(TT, r_rho, r_mrho)
            p.op("vector", lambda e: e.tensor_mul(dd.ap(), x0.ap(), bcast_g(r_rho)),
                 reads=[x0, r_rho], writes=[dd])
            p.op("vector", lambda e: e.tensor_sub(dd.ap(), dd.ap(), bcast_g(r_mrho)),
                 reads=[dd, r_mrho], writes=[dd])
            def _gb2(e):
                for g in range(G):
                    inst = e.tensor_scalar(
                        hc.ap()[:, g, 1:TT + 1], dd.ap()[:, g, :],
                        cvs(CV["g2"])[:, g:g + 1], cvs(CV["b2"])[:, g:g + 1], OP.mult, OP.add)
                return inst
            p.op("vector", _gb2, reads=[dd, cv], writes=[hc])
            if i == 0:
                p.op("vector", lambda e: e.tensor_copy(hc.ap()[:, :, 0:1], h2bd.ap().unsqueeze(2)),
                     reads=[h2bd, hc], writes=[hc])
            else:
                p.op("vector", lambda e: e.tensor_copy(hc.ap()[:, :, 0:1], hp.ap()[:, :, TT:TT + 1]),
                     reads=[hp, hc], writes=[hc])
            p.op("vector", lambda e: e.tensor_sub(
                dd.ap(), hc.ap()[:, :, 1:TT + 1], hc.ap()[:, :, 0:TT]), reads=[hc], writes=[dd])
            for dst, nm in ((xvb, "ftmk"), (xrb, "ftmr")):
                def _mix2(e, dst=dst, nm=nm):
                    for g in range(G):
                        inst = e.scalar_tensor_tensor(
                            dst.ap()[:, g, :], dd.ap()[:, g, :], cvs(CV[nm])[:, g:g + 1],
                            hc.ap()[:, g, 0:TT], OP.mult, OP.add)
                    return inst
                p.op("vector", _mix2, reads=[dd, hc, cv], writes=[dst])
            p.op("scalar", lambda e: e.dma_start(out=xk2_s3[:, :, t0:t0 + TT], in_=xvb.ap()),
                 reads=[xvb], dma=True)
            for dg in range(G):
                ps = ps_m0 if dg % 2 == 0 else ps_m1
                def _rmm(e, ps=ps, dg=dg):
                    for cg in range(G):
                        inst = e.matmul(ps.ap(), wfr_sb.ap()[:, cg, dg * 128:(dg + 1) * 128],
                                        xrb.ap()[:, cg, :], start=(cg == 0), stop=(cg == G - 1))
                    return inst
                p.op("tensor", _rmm, reads=[wfr_sb, xrb], writes=[ps])
                p.op("scalar", lambda e, ps=ps, dg=dg: e.activation(
                    ek.ap()[:, dg, :], ps.ap(), AF.Exp, scale=-1.0), reads=[ps], writes=[ek])
            p.op("scalar", lambda e: e.activation(sr.ap(), ek.ap(), AF.Ln, bias=1.0),
                 reads=[ek, c_one], writes=[sr])
            sb32 = Buf(big32.ap()[:, 0:G, :], "sr2b", track=big32)
            p.op("scalar", lambda e, d=sb32: e.activation(d.ap(), sr.ap(), AF.Exp, scale=-1.0),
                 reads=[sr], writes=[big32])
            p.op("scalar", lambda e, s=sb32: e.dma_start(out=sr2_s3[:, :, t0:t0 + TT], in_=s.ap()),
                 reads=[big32], dma=True)

        for i in range(NT):
            ph2a_tile(i, i * TT, h[i % 2], h[(i + 1) % 2])

        if "xk2" in self.debug_dump:
            o1_ = self.dram_out("dbg_xk2", (C, TOWN), BF16)
            p.barrier()
            p.op("sync", lambda e, o=o1_: e.dma_start(out=o, in_=xk2_s[:, :]), dma=True)
        if "sr2" in self.debug_dump:
            o2_ = self.dram_out("dbg_sr2", (C, TOWN), BF16)
            p.barrier()
            p.op("sync", lambda e, o=o2_: e.dma_start(out=o, in_=sr2_s[:, :]), dma=True)
        if stop_after == "ph2a":
            self._finish(p, sems)
            return nc

        p.barrier()
        _wfk3 = wfkT.rearrange("(g p) d -> p g d", p=128)
        for _q in range(4):
            p.op("gpsimd", lambda e, q=_q: e.dma_start(
                out=ware.ap()[:, :, q * 1024:(q + 1) * 1024],
                in_=_wfk3[:, :, q * 1024:(q + 1) * 1024]),
                writes=[ware], dma=True)

        rot6 = p4 + [ps_m0, ps_m1]
        rot5 = p4 + [ps_m0]
        rl_sl = [Buf(ktd[0].ap()[:, j, :], f"rl{j}") for j in range(4)]
        st_ch = [xsqb, x0b]
        st_sl = [[Buf(st_ch[c].ap()[:, j, :], f"st{c}_{j}") for j in range(G)]
                 for c in range(2)]

        def ph2b1_tile(i, t0, xk2t):
            p.op("sync", lambda e, x=xk2t: e.dma_start(
                out=x.ap(), in_=xk2_s3[:, :, t0:t0 + TT]), writes=[xk2t], dma=True)
            for fg in range(FG):
                ps = rot6[fg % 6]
                def _kmm(e, ps=ps, fg=fg, x=xk2t):
                    for cg in range(G):
                        inst = e.matmul(ps.ap(), ware.ap()[:, cg, fg * 128:(fg + 1) * 128],
                                        x.ap()[:, cg, :], start=(cg == 0), stop=(cg == G - 1))
                    return inst
                p.op("tensor", _kmm, reads=[ware, xk2t], writes=[ps])
                rl = rl_sl[fg % 4]
                p.op("scalar", lambda e, ps=ps, rl=rl: e.activation(
                    rl.ap(), ps.ap(), AF.Relu), reads=[ps], writes=[rl])
                ch = (fg // G) % 2
                dst = st_sl[ch][fg % G]
                p.op("vector" if ch == 0 else "gpsimd",
                     lambda e, rl=rl, dst=dst: e.tensor_mul(
                         dst.ap(), rl.ap(), rl.ap()), reads=[rl], writes=[dst, st_ch[ch]])
                if fg % G == G - 1:
                    p.op("scalar", lambda e, fg=fg, ch=ch: e.dma_start(
                        out=kk_s3[:, fg - G + 1:fg + 1, t0:t0 + TT], in_=st_ch[ch].ap()),
                        reads=st_sl[ch] + [st_ch[ch]], dma=True)

        for i in range(NT):
            ph2b1_tile(i, i * TT, xvb if i % 2 == 0 else xrb)
        if "kk" in self.debug_dump:
            o4_ = self.dram_out("dbg_kk", (F, TOWN), BF16)
            p.barrier()
            for _c in range(8):
                p.op("sync", lambda e, o=o4_, c=_c: e.dma_start(
                    out=o[c * 512:(c + 1) * 512, :], in_=kk_s[c * 512:(c + 1) * 512, :]), dma=True)

        p.barrier()
        wfv32 = Buf(ware.ap().rearrange("p g (x d) -> p (g x) d", x=4), "wfv32", track=ware)
        _wfv3 = wfvT.rearrange("(q p) d -> p q d", p=128)
        for _q in range(4):
            p.op("gpsimd", lambda e, q=_q: e.dma_start(
                out=wfv32.ap()[:, q * 8:(q + 1) * 8, :],
                in_=_wfv3[:, q * 8:(q + 1) * 8, :]),
                writes=[ware], dma=True)
        if "kv" in self.debug_dump:
            dbg_kv = self.dram_out("dbg_kv", (C, TOWN)).rearrange("(g p) t -> p g t", p=128)
        if "x2" in self.debug_dump:
            dbg_x2 = self.dram_out("dbg_x2", (C, TOWN)).rearrange("(g p) t -> p g t", p=128)

        def ph2b2_tile(i, t0):
            x1t_ = kt; x2t = vt; x0t = ek
            sr2t = x0b
            x2sq = xsqb
            p.op("sync", lambda e: e.dma_start(out=x1t_.ap(), in_=x1v[:, :, 1 + t0:1 + t0 + TT]),
                 writes=[x1t_], dma=True)
            p.op("sync", lambda e, s=sr2t: e.dma_start(out=s.ap(), in_=sr2_s3[:, :, t0:t0 + TT]),
                 writes=[sr2t], dma=True)
            xsrc = x_own.rearrange("(g p) t -> p g t", p=128)
            p.op("sync", lambda e, xs=xsrc: e.dma_start(out=x0t.ap(), in_=xs[:, :, t0:t0 + TT]),
                 writes=[x0t], dma=True)
            kkt = Buf(big32.ap()[:, 0:32, :], "kkt", track=big32)
            p.op("sync", lambda e, k=kkt: e.dma_start(
                out=k.ap(), in_=kk_s3[:, :, t0:t0 + TT]), writes=[big32], dma=True)
            for cg in range(G):
                ps = rot6[cg % 6]
                def _vmm(e, cg=cg, ps=ps, k=kkt):
                    for fg in range(FG):
                        inst = e.matmul(ps.ap(), wfv32.ap()[:, fg, cg * 128:(cg + 1) * 128],
                                        k.ap()[:, fg, :], start=(fg == 0), stop=(fg == FG - 1))
                    return inst
                p.op("tensor", _vmm, reads=[wfv32, big32], writes=[ps])
                if "kv" in self.debug_dump:
                    p.op("vector", lambda e, cg=cg, ps=ps: e.tensor_copy(
                        hin.ap()[:, cg, :], ps.ap()), reads=[ps, hin], writes=[hin])
                p.op("vector", lambda e, s=sr2t, cg=cg, ps=ps: e.tensor_mul(
                    x2t.ap()[:, cg, :], ps.ap(), s.ap()[:, cg, :]),
                    reads=[ps, sr2t, x2t], writes=[x2t])
            if "kv" in self.debug_dump:
                p.op("scalar", lambda e, t=t0: e.dma_start(
                    out=dbg_kv[:, :, t:t + TT], in_=hin.ap()), reads=[hin], dma=True)
            p.op("gpsimd", lambda e: e.tensor_add(x2t.ap(), x2t.ap(), x1t_.ap()),
                 reads=[x2t, x1t_], writes=[x2t])
            if "x2" in self.debug_dump:
                p.op("scalar", lambda e, t=t0: e.dma_start(
                    out=dbg_x2[:, :, t:t + TT], in_=x2t.ap()), reads=[x2t], dma=True)
            x2b = Buf(big32.ap()[:, 0:G, :], "x2b", track=big32)
            p.op("gpsimd", lambda e, d=x2b: e.tensor_copy(d.ap(), x2t.ap()), reads=[x2t, big32], writes=[big32])
            p.op("scalar", lambda e, d=x2sq: e.activation(d.ap(), x2t.ap(), AF.Square),
                 reads=[x2t], writes=[x2sq])
            stats_n(x2b, x2sq, TT, ps_S, ps_Q)
            rows_plain(TT, r_rho, r_mrho)
            p.op("vector", lambda e: e.tensor_mul(dd.ap(), x2t.ap(), bcast_g(r_rho)),
                 reads=[x2t, r_rho], writes=[dd])
            p.op("vector", lambda e: e.tensor_sub(dd.ap(), dd.ap(), bcast_g(r_mrho)),
                 reads=[dd, r_mrho], writes=[dd])
            p.op("gpsimd", lambda e: e.tensor_add(dd.ap(), dd.ap(), x0t.ap()),
                 reads=[dd, x0t], writes=[dd])
            p.op("scalar", lambda e: e.dma_start(out=out1_s3[:, :, t0:t0 + TT], in_=dd.ap()),
                 reads=[dd], dma=True)

        for i in range(NT):
            ph2b2_tile(i, i * TT)

        if "out1" in self.debug_dump:
            o3_ = self.dram_out("dbg_out1", (C, TOWN))
            p.barrier()
            p.op("sync", lambda e, o=o3_: e.dma_start(out=o, in_=out1_s[:, :]), dma=True)
        if stop_after == "ph2":
            self._finish(p, sems)
            return nc

        mscal = {}

        def wquant(w_h, nrow_groups, name, base):
            nchunk = nrow_groups
            ncols = 4096 if nrow_groups == G else 1024
            cw = min(2048, ncols)
            wview3 = w_h.rearrange("(g p) d -> p g d", p=128)
            acc = r_v0
            stg = [(Buf(x0.ap().rearrange("p g t -> p (g t)"), "sflat", track=x0), x0),
                   (Buf(dd.ap().rearrange("p g t -> p (g t)"), "sflat2", track=dd), dd)]
            QR = ["sync", "gpsimd", "scalar"]
            absout = Buf(big32.ap().rearrange("p q t -> p (q t)"), "absout", track=big32)
            nsub = ncols // cw
            idx = 0
            for gch in range(nchunk):
                for s in range(nsub):
                    sf, sb = stg[idx % 2]
                    p.op(QR[idx % 3], lambda e, gch=gch, s=s, cw=cw, sf=sf: e.dma_start(
                        out=sf.ap()[:, 0:cw], in_=wview3[:, gch, s * cw:(s + 1) * cw]),
                        writes=[sb], dma=True)
                    p.op("scalar", lambda e, idx=idx, cw=cw, sf=sf: e.activation(
                        absout.ap()[:, 0:cw], sf.ap()[:, 0:cw], AF.Abs,
                        accum_out=acc.ap()[:, idx:idx + 1]),
                        reads=[sb], writes=[big32, acc])
                    idx += 1
            nacc = idx
            p.op("vector", lambda e, n=nacc: e.tensor_reduce(
                rmax.ap()[:, 0:1], acc.ap()[:, 0:n], mybir.AxisListType.X, OP.add),
                reads=[acc], writes=[rmax])
            if "wq_row" not in self.sb:
                self.sb["wq_row"] = self.sbuf("wq_row", (1, 128))
                self.sb["wq_thr"] = self.sbuf("wq_thr", (128, 2))
            rowt = self.sb["wq_row"]
            def _rowt(e):
                with self.nc.allow_non_contiguous_dma(reason="128x4B gather"):
                    return e.dma_start(out=rowt.ap(), in_=rmax.ap()[:, 0:1])
            p.op("gpsimd", _rowt, reads=[rmax], writes=[rowt], dma=True)
            msc = self.sbuf(f"wq_m_{name}", (1, 1))
            p.op("vector", lambda e: e.tensor_reduce(
                msc.ap(), rowt.ap(), mybir.AxisListType.X, OP.add), reads=[rowt], writes=[msc])
            cnt = float(nrow_groups * 128 * ncols)
            p.op("vector", lambda e, c=cnt: e.tensor_scalar(
                msc.ap(), msc.ap(), 1.0 / c, 1e-5, OP.mult, OP.max), reads=[msc], writes=[msc])
            mp = self.sbuf(f"wq_mp_{name}", (128, 1))
            p.op("tensor", lambda e: e.matmul(ps_m0.ap()[:, 0:1], ones_f1.ap(), msc.ap(),
                                              start=True, stop=True),
                 reads=[ones_f1, msc], writes=[ps_m0])
            p.op("vector", lambda e: e.tensor_copy(mp.ap(), ps_m0.ap()[:, 0:1]),
                 reads=[ps_m0], writes=[mp])
            thr = self.sb["wq_thr"]
            p.op("vector", lambda e: e.tensor_scalar_mul(thr.ap()[:, 0:1], mp.ap(), 0.5),
                 reads=[mp], writes=[thr])
            p.op("vector", lambda e: e.tensor_scalar_mul(thr.ap()[:, 1:2], thr.ap()[:, 0:1], -1.0),
                 reads=[thr], writes=[thr])
            wareflat = Buf(ware.ap().rearrange("p g d -> p (g d)"), "wareflat", track=ware)
            idx = 0
            for gch in range(nchunk):
                for s in range(nsub):
                    sf, sb = stg[idx % 2]
                    p.op(QR[idx % 3], lambda e, gch=gch, s=s, cw=cw, sf=sf: e.dma_start(
                        out=sf.ap()[:, 0:cw], in_=wview3[:, gch, s * cw:(s + 1) * cw]),
                        writes=[sb], dma=True)
                    off = (gch * ncols + s * cw)
                    p.op("vector", lambda e, off=off, cw=cw, sf=sf, sb=sb: e._custom_dve(
                        self.ops["ATH_TERN"], out=wareflat.ap()[:, off:off + cw],
                        in0=sf.ap()[:, 0:cw], s0=thr.ap()[:, 0:1], s1=thr.ap()[:, 1:2]),
                        reads=[sb, thr], writes=[ware])
                    idx += 1
            mscal[name] = (mp, msc)
            return mp, msc

        p.barrier()
        mg_p, mg_s = wquant(wgT, G, "g", 0)
        amax_tok = self.sbuf("amax_tok", (128, 8))
        p.op("vector", lambda e: e.memset(amax_tok.ap(), 0.0), writes=[amax_tok])
        rows8 = self.sbuf("rows8", (8, 128))
        row1 = self.sbuf("row1", (1, 2 * TT))
        ps_bc = ps_S
        ps_tp = ps_m1

        def amax_machinery(Mb, pre_ops, deq_ops, qrow_t0=None):
            for c in range(2):
                p.op("tensor", lambda e, c=c: e.transpose(
                    ps_tp.ap()[:, 0:128], Mb.ap()[:, c * 128:(c + 1) * 128], ident.ap()),
                    reads=[Mb, ident], writes=[ps_tp])
                p.op("vector", lambda e, c=c: e.tensor_reduce(
                    amax_tok.ap()[:, c:c + 1], ps_tp.ap()[:, 0:128], mybir.AxisListType.X, OP.max),
                    reads=[ps_tp], writes=[amax_tok])
            for f in pre_ops:
                p.op("vector", f, reads=[amax_tok, mg_p, mu_p] if mu_p is not None else [amax_tok],
                     writes=[amax_tok])
            p.op("vector", lambda e: e.reciprocal(
                amax_tok.ap()[:, 4:6], amax_tok.ap()[:, 0:2]), reads=[amax_tok], writes=[amax_tok])
            p.op("vector", lambda e: e.drain(), reads=[amax_tok])
            for f in deq_ops:
                p.op("vector", f, reads=[amax_tok, mg_p, mu_p] if mu_p is not None else [amax_tok],
                     writes=[amax_tok])
            p.op("tensor", lambda e: e.transpose(
                ps_tp.ap()[0:8, 0:128], amax_tok.ap()[:, 0:8], ident.ap()),
                reads=[amax_tok, ident], writes=[ps_tp])
            p.op("vector", lambda e: e.tensor_copy(rows8.ap(), ps_tp.ap()[0:8, 0:128]),
                 reads=[ps_tp], writes=[rows8])
            def _row_inv(e):
                with self.nc.allow_non_contiguous_dma(reason="row gather"):
                    return e.dma_start(out=row1.ap()[:, 0:TT], in_=rows8.ap()[4:6, :])
            p.op("gpsimd", _row_inv, reads=[rows8], writes=[row1], dma=True)
            def _row_deq(e):
                with self.nc.allow_non_contiguous_dma(reason="row gather"):
                    return e.dma_start(out=row1.ap()[:, TT:2 * TT], in_=rows8.ap()[2:4, :])
            p.op("gpsimd", _row_deq, reads=[rows8], writes=[row1], dma=True)
            if qrow_t0 is not None:
                def _row_q(e, t0=qrow_t0):
                    with self.nc.allow_non_contiguous_dma(reason="row store"):
                        return e.dma_start(out=qrow_s[0:1, t0:t0 + TT], in_=rows8.ap()[2:4, :])
                p.op("gpsimd", _row_q, reads=[rows8], dma=True)
            p.op("tensor", lambda e: e.matmul(ps_bc.ap(), ones_f1.ap(), row1.ap()[0:1, 0:TT],
                                              start=True, stop=True),
                 reads=[ones_f1, row1], writes=[ps_bc])
            p.op("tensor", lambda e: e.matmul(ps_Q.ap(), ones_f1.ap(), row1.ap()[0:1, TT:2 * TT],
                                              start=True, stop=True),
                 reads=[ones_f1, row1], writes=[ps_Q])
            return ps_bc, ps_Q

        mu_p = None

        def ph3ai_tile(i, t0):
            h2o = kt; yq = vt; rp = ek; rq = ekv
            p.op("sync", lambda e: e.dma_start(out=x0.ap(), in_=out1_s3[:, :, t0:t0 + TT]),
                 writes=[x0], dma=True)
            p.op("gpsimd", lambda e: e.tensor_copy(x0b.ap(), x0.ap()), reads=[x0], writes=[x0b])
            p.op("scalar", lambda e: e.activation(xsqb.ap(), x0.ap(), AF.Square), reads=[x0], writes=[xsqb])
            stats_n(x0b, xsqb, TT, ps_S, ps_Q)
            rows_plain(TT, r_rho, r_mrho)
            p.op("vector", lambda e: e.tensor_mul(h2o.ap(), x0.ap(), bcast_g(r_rho)),
                 reads=[x0, r_rho], writes=[h2o])
            p.op("vector", lambda e: e.tensor_sub(h2o.ap(), h2o.ap(), bcast_g(r_mrho)),
                 reads=[h2o, r_mrho], writes=[h2o])
            p.op("vector", lambda e: e.tensor_reduce(
                r_v0.ap().unsqueeze(2), h2o.ap().transpose([0, 2, 1]), mybir.AxisListType.X,
                OP.max, apply_absolute_value=True), reads=[h2o], writes=[r_v0])
            inv_b, deq_b = amax_machinery(
                r_v0,
                [lambda e: e.tensor_scalar_max(amax_tok.ap()[:, 0:2], amax_tok.ap()[:, 0:2], 1e-5)],
                [lambda e: e.tensor_scalar_mul(amax_tok.ap()[:, 4:6], amax_tok.ap()[:, 4:6], 127.0),
                 lambda e: e.tensor_scalar_mul(amax_tok.ap()[:, 2:4], amax_tok.ap()[:, 0:2], 1.0 / 127.0)])
            p.op("vector", lambda e, ib=inv_b: e.tensor_mul(
                yq.ap(), h2o.ap(), ib.ap().unsqueeze(1).broadcast_to([128, G, TT])),
                reads=[h2o, inv_b], writes=[yq])
            p.op("vector", lambda e: e._custom_dve(self.ops["ATH_RINT"], out=rq.ap(), in0=yq.ap(),
                                                   s0=12582912.0, s1=12582912.0), reads=[yq], writes=[rq])
            p.op("vector", lambda e, db=deq_b: e.tensor_mul(
                xvb.ap(), rq.ap(), db.ap().unsqueeze(1).broadcast_to([128, G, TT])),
                reads=[rq, deq_b], writes=[xvb])
            p.op("scalar", lambda e: e.dma_start(out=xhat1_s3[:, :, t0:t0 + TT], in_=xvb.ap()),
                 reads=[xvb], dma=True)
            for fg in range(FG):
                ps = rot5[fg % 5]
                def _gmm(e, ps=ps, fg=fg):
                    for cg in range(G):
                        inst = e.matmul(ps.ap(), ware.ap()[:, cg, fg * 128:(fg + 1) * 128],
                                        xvb.ap()[:, cg, :], start=(cg == 0), stop=(cg == G - 1))
                    return inst
                p.op("tensor", _gmm, reads=[ware, xvb], writes=[ps])
                ch = (fg // G) % 2
                dst = st_sl[ch][fg % G]
                p.op("scalar", lambda e, ps=ps, dst=dst: e.activation(
                    dst.ap(), ps.ap(), AF.Copy), reads=[ps], writes=[dst, st_ch[ch]])
                if fg % G == G - 1:
                    p.op("scalar", lambda e, fg=fg, ch=ch: e.dma_start(
                        out=gpre_s3[:, fg - G + 1:fg + 1, t0:t0 + TT], in_=st_ch[ch].ap()),
                        reads=st_sl[ch] + [st_ch[ch]], dma=True)

        for i in range(NT):
            ph3ai_tile(i, i * TT)
        if stop_after == "ph3ai":
            self._finish(p, sems)
            return nc

        p.barrier()
        mu_p, mu_s = wquant(wuT, G, "u", 64)
        md_row = self.sbuf("md_row", (1, 128))

        gu_sl = [Buf(big32.ap()[:, q, :], f"gu_{q}") for q in range(FG)]
        upv_sl = [Buf(ktd[0].ap()[:, 4 + j, :], f"upv{j}") for j in range(4)]

        def ph3aii_tile(i, t0):
            xh1 = x0b
            p.op("sync", lambda e: e.dma_start(out=xh1.ap(), in_=xhat1_s3[:, :, t0:t0 + TT]),
                 writes=[xh1], dma=True)
            p.op("sync", lambda e: e.dma_start(out=big32.ap(), in_=gpre_s3[:, :, t0:t0 + TT]),
                 writes=[big32] + gu_sl, dma=True)
            for fg in range(FG):
                ps = rot5[fg % 5]
                def _umm(e, ps=ps, fg=fg):
                    for cg in range(G):
                        inst = e.matmul(ps.ap(), ware.ap()[:, cg, fg * 128:(fg + 1) * 128],
                                        xh1.ap()[:, cg, :], start=(cg == 0), stop=(cg == G - 1))
                    return inst
                p.op("tensor", _umm, reads=[ware, xh1], writes=[ps])
                sil, upv = rl_sl[fg % 4], upv_sl[fg % 4]
                p.op("scalar", lambda e, fg=fg, sil=sil: e.activation(
                    sil.ap(), big32.ap()[:, fg, :], AF.Silu, scale=mg_p.ap()),
                    reads=[big32, mg_p], writes=[sil])
                p.op("scalar", lambda e, ps=ps, upv=upv: e.activation(
                    upv.ap(), ps.ap(), AF.Copy), reads=[ps], writes=[upv])
                p.op("gpsimd", lambda e, fg=fg, sil=sil, upv=upv: e.tensor_mul(
                    big32.ap()[:, fg, :], sil.ap(), upv.ap()),
                    reads=[sil, upv], writes=[gu_sl[fg]])
            p.op("vector", lambda e: e.tensor_reduce(
                r_v0.ap().unsqueeze(2), big32.ap().transpose([0, 2, 1]), mybir.AxisListType.X,
                OP.max, apply_absolute_value=True), reads=gu_sl, writes=[r_v0])
            inv_b, deq_b = amax_machinery(
                r_v0,
                [lambda e: e.tensor_scalar_mul(amax_tok.ap()[:, 0:2], amax_tok.ap()[:, 0:2], mu_p.ap()),
                 lambda e: e.tensor_scalar_max(amax_tok.ap()[:, 0:2], amax_tok.ap()[:, 0:2], 1e-5)],
                [lambda e: e.tensor_scalar(amax_tok.ap()[:, 4:6], amax_tok.ap()[:, 4:6],
                                           mu_p.ap(), 127.0, OP.mult, OP.mult),
                 lambda e: e.tensor_scalar_mul(amax_tok.ap()[:, 2:4], amax_tok.ap()[:, 0:2], 1.0 / 127.0)],
                qrow_t0=t0)
            for ch in range(4):
                yq = vt
                stage = xsqb if ch % 2 == 0 else x0b
                p.op("vector", lambda e, ch=ch, ib=inv_b: e.tensor_mul(
                    yq.ap(), big32.ap()[:, 8 * ch:8 * ch + 8, :],
                    ib.ap().unsqueeze(1).broadcast_to([128, G, TT])),
                    reads=gu_sl[8 * ch:8 * ch + 8] + [inv_b], writes=[yq])
                p.op("vector", lambda e, stage=stage: e._custom_dve(
                    self.ops["ATH_RINT"], out=stage.ap(), in0=yq.ap(),
                    s0=12582912.0, s1=12582912.0), reads=[yq], writes=[stage])
                p.op("scalar", lambda e, ch=ch, stage=stage: e.dma_start(
                    out=xq2_s3[:, 8 * ch:8 * ch + 8, t0:t0 + TT], in_=stage.ap()),
                    reads=[stage], dma=True)

        for i in range(NT):
            ph3aii_tile(i, i * TT)
        if stop_after == "ph3aii":
            self._finish(p, sems)
            return nc

        p.barrier()
        md_p, md_s = wquant(wdT, FG, "d", 128)
        wd32 = Buf(ware.ap().rearrange("p g (x d) -> p (g x) d", x=4), "wd32", track=ware)
        out_3 = out.rearrange("(g p) t -> p g t", p=128)
        mdf = self.sbuf("mdf", (1, 1))
        def ph3b_tile(i, t0):
            ft = dd; o1t = ek; tmp = kt
            xqt = Buf(big32.ap()[:, 0:32, :], "xqt", track=big32)
            p.op("sync", lambda e, k=xqt: e.dma_start(
                out=k.ap(), in_=xq2_s3[:, :, t0:t0 + TT]), writes=[big32], dma=True)
            for cg in range(G):
                ps = ps_m0 if cg % 2 == 0 else ps_m1
                def _dmm(e, cg=cg, ps=ps, k=xqt):
                    for fg in range(FG):
                        inst = e.matmul(ps.ap(), wd32.ap()[:, fg, cg * 128:(cg + 1) * 128],
                                        k.ap()[:, fg, :], start=(fg == 0), stop=(fg == FG - 1))
                    return inst
                p.op("tensor", _dmm, reads=[wd32, big32], writes=[ps])
                p.op("vector", lambda e, cg=cg, ps=ps: e.tensor_copy(
                    ps_kv.ap()[:, cg, :], ps.ap()), reads=[ps, ps_kv], writes=[ps_kv])
            p.op("sync", lambda e: e.dma_start(out=row1.ap()[:, 0:TT], in_=qrow_s[0:1, t0:t0 + TT]),
                 writes=[row1], dma=True)
            p.op("vector", lambda e: e.tensor_scalar_mul(row1.ap()[:, 0:TT], row1.ap()[:, 0:TT],
                                                         md_s.ap()), reads=[row1, md_s], writes=[row1])
            p.op("tensor", lambda e: e.matmul(ps_bc.ap(), ones_f1.ap(), row1.ap()[0:1, 0:TT],
                                              start=True, stop=True),
                 reads=[ones_f1, row1], writes=[ps_bc])
            p.op("scalar", lambda e: e.activation(r_msq.ap(), ps_bc.ap(), AF.Copy),
                 reads=[ps_bc], writes=[r_msq])
            p.op("vector", lambda e: e.tensor_mul(
                ft.ap(), ps_kv.ap(), r_msq.ap().unsqueeze(1).broadcast_to([128, G, TT])),
                reads=[ps_kv, r_msq], writes=[ft])
            fb = Buf(big32.ap()[:, 0:G, :], "fb", track=big32)
            fsq = Buf(big32.ap()[:, 8:16, :], "fsq", track=big32)
            p.op("gpsimd", lambda e, d=fb: e.tensor_copy(d.ap(), ft.ap()), reads=[ft], writes=[big32])
            p.op("scalar", lambda e, d=fsq: e.activation(d.ap(), ft.ap(), AF.Square),
                 reads=[ft], writes=[big32])
            stats_n(fb, fsq, TT, ps_S, ps_Q)
            rows_plain(TT, r_rho, r_mrho)
            p.op("sync", lambda e: e.dma_start(out=o1t.ap(), in_=out1_s3[:, :, t0:t0 + TT]),
                 writes=[o1t], dma=True)
            p.op("vector", lambda e: e.tensor_mul(tmp.ap(), ft.ap(), bcast_g(r_rho)),
                 reads=[ft, r_rho], writes=[tmp])
            p.op("vector", lambda e: e.tensor_sub(tmp.ap(), tmp.ap(), bcast_g(r_mrho)),
                 reads=[tmp, r_mrho], writes=[tmp])
            p.op("gpsimd", lambda e: e.tensor_add(tmp.ap(), tmp.ap(), o1t.ap()),
                 reads=[tmp, o1t], writes=[tmp])
            p.op("scalar", lambda e: e.dma_start(out=out_3[:, :, t0:t0 + TT], in_=tmp.ap()),
                 reads=[tmp], dma=True)

        if "ph3dbg" in self.debug_dump:
            p.barrier()
            o5_ = self.dram_out("dbg_xhat1", (C, TOWN), BF16)
            p.op("sync", lambda e, o=o5_: e.dma_start(out=o, in_=xhat1_s[:, :]), dma=True)
            o6_ = self.dram_out("dbg_gpre", (F, TOWN), BF16)
            for _c in range(8):
                p.op("sync", lambda e, o=o6_, c=_c: e.dma_start(
                    out=o[c * 512:(c + 1) * 512, :], in_=gpre_s[c * 512:(c + 1) * 512, :]), dma=True)
            o7_ = self.dram_out("dbg_xq2", (F, TOWN), BF16)
            for _c in range(8):
                p.op("sync", lambda e, o=o7_, c=_c: e.dma_start(
                    out=o[c * 512:(c + 1) * 512, :], in_=xq2_s[c * 512:(c + 1) * 512, :]), dma=True)
            o8_ = self.dram_out("dbg_qrow", (1, TOWN))
            p.op("sync", lambda e, o=o8_: e.dma_start(out=o, in_=qrow_s[:, :]), dma=True)

        for i in range(NT):
            ph3b_tile(i, i * TT)

        self._finish(p, sems)
        return nc

    def ph2_ph3(self, p, sems, env):
        pass

    def _finish(self, p, sems):
        nc = self.nc
        snap = dict(p.cnt)
        p.q["sync"].append((None, {s: c for s, c in snap.items() if c > 0}, None, 0, "final"))
        import contextlib
        es = contextlib.ExitStack()
        for s in list(ALL_DMA_SEMS) + ["v", "a", "t", "p", "sq"]:
            sems[s] = es.enter_context(nc.semaphore(s))
        block = es.enter_context(nc.Block())
        with es:
            p.emit(block, sems)
        mybir.codegen_inst_isa_subclasses(nc)
        return nc

def kernel(**inputs):
    inp = {k: np.asarray(v) for k, v in inputs.items()}
    x = inp["L0_attn_x"]
    B, T, Cdim = x.shape
    assert (B, T, Cdim) == (4, 4096, 1024)

    cv = []
    for nm in ("tmk", "tmv", "tmr"):
        v = inp[f"L0_attn_{nm}"].astype(np.float32)
        cv.append(1.0 - v); cv.append(v)
    cv.append(inp["L0_attn_ln1_w"]); cv.append(inp["L0_attn_ln1_b"])
    cv.append(inp["L0_attn_ln2_w"]); cv.append(inp["L0_attn_ln2_b"])
    for nm in ("f_tmk", "f_tmr"):
        v = inp[f"L0_attn_{nm}"].astype(np.float32)
        cv.append(1.0 - v); cv.append(v)
    cv.append(inp["L0_attn_time_decay"]); cv.append(inp["L0_attn_time_first"])
    cvec = np.concatenate([np.asarray(v, np.float32).reshape(8, 128).T for v in cv],
                          axis=1)
    cvec = np.ascontiguousarray(cvec)

    shared = {
        "iden": np.eye(128, dtype=np.float32),
        "cvec": cvec,
        "wkT": np.ascontiguousarray(inp["L0_attn_att_kw"].T),
        "wvT": np.ascontiguousarray(inp["L0_attn_att_vw"].T),
        "wrT": np.ascontiguousarray(inp["L0_attn_att_rw"].T),
        "woT": np.ascontiguousarray(inp["L0_attn_att_ow"].T),
        "wfkT": np.ascontiguousarray(inp["L0_attn_f_kw"].T),
        "wfvT": np.ascontiguousarray(inp["L0_attn_f_vw"].T),
        "wfrT": np.ascontiguousarray(inp["L0_attn_f_rw"].T),
        "wgT": np.ascontiguousarray(inp["L0_ffn_w_gate"].T),
        "wuT": np.ascontiguousarray(inp["L0_ffn_w_up"].T),
        "wdT": np.ascontiguousarray(inp["L0_ffn_w_down"].T),
    }

    in_maps = []
    for core in range(8):
        b, half = core // 2, core % 2
        xt = np.ascontiguousarray(x[b].T)
        own = np.ascontiguousarray(xt[:, half * TOWN:(half + 1) * TOWN])
        pre = np.ascontiguousarray(xt[:, 0:TOWN]) if half == 1 else own
        m = dict(shared)
        m["x_pre"] = pre
        m["x_own"] = own
        m["gates"] = np.full((128, G), 0.0 if half == 1 else -1e30, np.float32)
        m["bgate"] = np.full((128, G), 1.0 if half == 1 else 0.0, np.float32)
        in_maps.append(m)

    mk = MK()
    nc = mk.build()
    res = run_bass_kernel_spmd(nc, in_maps, core_ids=list(range(8)))

    out = np.empty((B, T, Cdim), np.float32)
    for core in range(8):
        b, half = core // 2, core % 2
        shard = np.asarray(res.results[core]["out"])
        out[b, half * TOWN:(half + 1) * TOWN, :] = shard.T
    return out
